# version 24
# baseline (speedup 1.0000x reference)
import math
import os
import numpy as np
import concourse.bass as bass
import concourse.mybir as mybir
from concourse.bass_utils import run_bass_kernel_spmd

F32 = mybir.dt.float32
BF16 = mybir.dt.bfloat16
AF = mybir.ActivationFunctionType
ALU = mybir.AluOpType
AX = mybir.AxisListType

T, D = 2048, 1024
NH, HD, RD, KVR, QR = 16, 64, 32, 256, 512
NT, NDC = T // 128, D // 128
SCALE = float(np.float32(1.0) / np.sqrt(np.float32(96.0)))
LN8H = float(-0.5 * math.log(8.0))
E_, INNER = 8, 32

_DEBUG = False

ENGS = ("T", "V", "A", "G", "Y")
ENG_ATTR = {"T": "tensor", "V": "vector", "A": "scalar", "G": "gpsimd", "Y": "sync"}


def _host_consts():
    inv_freq = 1.0 / 10000.0 ** (np.arange(0, RD, 2, dtype=np.float32) / RD)
    t = np.arange(T, dtype=np.float32)
    freqs = np.outer(t, inv_freq)
    emb = np.concatenate([freqs, freqs], -1)
    cos = np.cos(emb).astype(np.float32)
    sin = np.sin(emb).astype(np.float32)
    cos_d = np.tile(np.ascontiguousarray(cos.T), (4, 1))
    sin_s = np.ascontiguousarray(sin.T).copy()
    sin_s[:RD // 2, :] *= -1.0
    sin_d = np.tile(sin_s, (4, 1))
    ident = np.eye(128, dtype=np.float32)
    dmask = np.triu(np.ones((128, 128), dtype=np.float32))
    sel = np.zeros((8, 256), dtype=np.float32)
    for e in range(8):
        sel[e, 32 * e:32 * e + 32] = 1.0
    return dict(cosd=np.ascontiguousarray(cos_d), sind=np.ascontiguousarray(sin_d),
                ident=ident, dmask=np.ascontiguousarray(dmask), sel=sel)


class Slot:
    __slots__ = ("arena", "off", "p", "f", "name")

    def __init__(self, arena, off, p, f, name):
        self.arena, self.off, self.p, self.f, self.name = arena, off, p, f, name

    def v(self, p0=0, p1=None, f0=0, f1=None):
        p1 = self.p if p1 is None else p1
        f1 = self.f if f1 is None else f1
        return self.arena[p0:p1, self.off + f0:self.off + f1]

    @property
    def ap(self):
        return self.v()


class Pool:
    def __init__(self, arena, total, name):
        self.arena, self.name, self.total = arena, name, total
        self.free_list = [(0, total)]
        self.pending = []
        self.peak = 0

    def alloc(self, p, f, name=""):
        for i, (off, sz) in enumerate(self.free_list):
            if sz >= f:
                if sz == f:
                    self.free_list.pop(i)
                else:
                    self.free_list[i] = (off + f, sz - f)
                self.peak = max(self.peak, self.total - sum(s for _, s in self.free_list))
                return Slot(self.arena, off, p, f, name)
        used = self.total - sum(sz for _, sz in self.free_list)
        raise MemoryError(f"pool {self.name} OOM for {name} [{p},{f}] "
                          f"used={used}/{self.total} free={self.free_list}")

    def free(self, *slots):
        self.pending.extend(slots)

    def reclaim(self):
        for s in self.pending:
            self.free_list.append((s.off, s.f))
        self.pending = []
        self.free_list.sort()
        merged = []
        for off, sz in self.free_list:
            if merged and merged[-1][0] + merged[-1][1] == off:
                merged[-1] = (merged[-1][0], merged[-1][1] + sz)
            else:
                merged.append([off, sz])
        self.free_list = [tuple(x) for x in merged]


class Bank:
    __slots__ = ("tensor", "foff", "name")

    def __init__(self, tensor, foff, name):
        self.tensor, self.foff, self.name = tensor, foff, name

    def v(self, p0=0, p1=128, f0=0, f1=512):
        return self.tensor[p0:p1, self.foff + f0:self.foff + f1]


class Prog:
    def __init__(self):
        self.items = []
        self.res_w = {}
        self.res_r = {}
        self.bar = None
        self.bar_seen = set()
        self.hard_all = False

    def emit(self, eng, fn, reads=(), writes=(), dma=False, tag="",
             special=False, wwidth=512, tail=False):
        hardp = special or wwidth < 512 or self.hard_all
        deps = set()
        hard = set()
        for r in reads:
            w = self.res_w.get(id(r))
            if w is not None:
                deps.add(w)
                if self.items[w]["hardp"]:
                    hard.add(w)
        for w_ in writes:
            w = self.res_w.get(id(w_))
            if w is not None:
                deps.add(w)
                if self.items[w]["hardp"]:
                    hard.add(w)
            deps.update(self.res_r.get(id(w_), ()))
        key = (eng, dma)
        if self.bar is not None and key not in self.bar_seen:
            deps.update(self.bar)
            self.bar_seen.add(key)
        idx = len(self.items)
        self.items.append(dict(eng=eng, fn=fn, deps=deps, hard=hard, dma=dma,
                               tag=tag, inc=False, val=None, hardp=hardp,
                               tail=tail))
        for r in reads:
            self.res_r.setdefault(id(r), []).append(idx)
        for w_ in writes:
            self.res_w[id(w_)] = idx
            self.res_r[id(w_)] = []
        return idx

    def barrier(self):
        last = {}
        for i, it in enumerate(self.items):
            last[(it["eng"], it["dma"])] = i
        self.bar = sorted(last.values())
        self.bar_seen = set()

    def finalize(self, block, sems):
        items = self.items
        for it in items:
            for d in it["deps"]:
                dit = items[d]
                if dit["dma"] or it["dma"] or dit["eng"] != it["eng"] \
                        or d in it["hard"]:
                    dit["inc"] = True
        for it in items:
            if it["dma"]:
                it["inc"] = True
        counters = {}
        dma_idx = {}
        for it in items:
            if it["dma"]:
                i = dma_idx.get(it["eng"], 0)
                dma_idx[it["eng"]] = i + 1
                it["lane"] = i % 4
                it["lidx"] = i // 4
                it["val"] = 16 * (i // 4 + 1)
            elif it["inc"]:
                key = (it["eng"], False)
                counters[key] = counters.get(key, 0) + 1
                it["val"] = counters[key]
        for eng in ENGS:
            stream = [it for it in items if it["eng"] == eng]
            if not stream:
                continue

            def body(e, stream=stream, eng=eng):
                water = {}
                for it in stream:
                    waits = {}
                    for d in it["deps"]:
                        dit = items[d]
                        same = (dit["eng"] == eng and not dit["dma"] and not it["dma"]
                                and d not in it["hard"])
                        if same or dit["val"] is None:
                            continue
                        if dit["dma"]:
                            skey = (dit["eng"], True, dit["lane"])
                        else:
                            skey = (dit["eng"], False)
                        waits[skey] = max(waits.get(skey, 0), dit["val"])
                    for skey in sorted(waits, key=str):
                        val = waits[skey]
                        if water.get(skey, 0) < val:
                            e.wait_ge(sems[skey], val)
                            water[skey] = val
                    if it["dma"] and not it["tail"] and it["lidx"] > 0:
                        skey = (eng, True, it["lane"])
                        prev = 16 * it["lidx"]
                        if water.get(skey, 0) < prev:
                            e.wait_ge(sems[skey], prev)
                            water[skey] = prev
                    inst = it["fn"](e)
                    if it["dma"]:
                        inst.then_inc(sems[(eng, True, it["lane"])], 16)
                    elif it["inc"]:
                        assert inst is not None, f"{it['tag']} returned None"
                        inst.then_inc(sems[(eng, False)], 1)
            getattr(block, ENG_ATTR[eng])(body)


def build_program(debug=False):
    nc = bass.Bass("TRN2", target_bir_lowering=False, debug=False,
                   detect_race_conditions=False)

    di = {}

    def dram_in(name, shape):
        di[name] = nc.dram_tensor(name, list(shape), F32, kind="ExternalInput").ap()
        return di[name]

    d_xT = dram_in("xT", (D, T))
    d_wkv = dram_in("wkv", (D, KVR))
    d_wq = dram_in("wq", (D, QR))
    d_wkrope = dram_in("wkrope", (D, QR))
    d_wqrope = dram_in("wqrope", (QR, QR))
    d_wkdec = dram_in("wkdec", (KVR, D))
    d_wv2 = dram_in("wv2", (KVR, NH * 65))
    d_wqdec = dram_in("wqdec", (QR, D))
    d_wo = dram_in("wo", (D, D))
    d_gkv = dram_in("gkv", (KVR, 1))
    d_wvT = dram_in("wvT", (D, D))
    d_bv = dram_in("bv", (1, D))
    d_gaterhs = dram_in("gaterhs", (D, 72))
    d_bg72 = dram_in("bg72", (1, 72))
    d_w1catT = dram_in("w1catT", (D, E_ * INNER))
    d_b1row = dram_in("b1row", (1, E_ * INNER))
    d_w2cat = dram_in("w2cat", (E_ * INNER, D))
    d_b2 = dram_in("b2", (E_, D))
    d_grcb = dram_in("grcb", (128, 32))
    d_alpha = dram_in("alphacol", (128, 1))
    d_cosd = dram_in("cosd", (128, T))
    d_sind = dram_in("sind", (128, T))
    d_ident = dram_in("ident", (128, 128))
    d_dmask = dram_in("dmask", (128, 128))
    d_sel = dram_in("sel", (E_, 256))

    d_out = nc.dram_tensor("out", [D, T], F32, kind="ExternalOutput").ap()
    tap_drams = {}

    P = Prog()
    P.hard_all = bool(int(os.environ.get("KHARD", "0")))
    from contextlib import ExitStack
    with ExitStack() as ctx:
        AR_B = ctx.enter_context(nc.sbuf_tensor("arena_b", [128, 88500], BF16))
        AR_F = ctx.enter_context(nc.sbuf_tensor("arena_f", [128, 8600], F32))
        PSA = ctx.enter_context(nc.psum_tensor("psA", [128, 1024], F32))
        PSB = ctx.enter_context(nc.psum_tensor("psB", [128, 1024], F32))
        PS47 = [ctx.enter_context(nc.psum_tensor(f"ps{i}", [128, 512], F32))
                for i in range(4, 8)]
        sems = {}
        for eng in ENGS:
            sems[(eng, False)] = ctx.enter_context(nc.semaphore(f"c{eng}"))
            for lane in range(4):
                sems[(eng, True, lane)] = ctx.enter_context(
                    nc.semaphore(f"d{eng}{lane}"))
        block = ctx.enter_context(nc.Block())

        pb = Pool(AR_B, 88500, "bf16")
        pf = Pool(AR_F, 8600, "f32")
        banks = [Bank(PSA, 0, "b0"), Bank(PSA, 512, "b1"),
                 Bank(PSB, 0, "b2"), Bank(PSB, 512, "b3"),
                 Bank(PS47[0], 0, "b4"), Bank(PS47[1], 0, "b5"),
                 Bank(PS47[2], 0, "b6"), Bank(PS47[3], 0, "b7")]
        span_t = {0: PSA, 1: PSB}

        def ld(dst, src_ap, tag="ld", p=None):
            pp = dst.p if p is None else p
            P.emit("G", lambda e, d=dst, s=src_ap, pp=pp: e.dma_start(out=d.v(0, pp), in_=s),
                   writes=(dst,), dma=True, tag=tag)

        def ldY(dst, src_ap, tag="ldY", p=None):
            pp = dst.p if p is None else p
            P.emit("Y", lambda e, d=dst, s=src_ap, pp=pp: e.dma_start(out=d.v(0, pp), in_=s),
                   writes=(dst,), dma=True, tag=tag)

        def mm(bank_, out_fn, lhs_fn, rhs_fn, start, stop, reads, tag="mm"):
            P.emit("T", lambda e: e.matmul(out_fn(), lhs_fn(), rhs_fn(),
                                           start=start, stop=stop),
                   reads=list(reads) + [bank_], writes=(bank_,), tag=tag)

        ev_rr = [0]

        def ev(dst_ap_fn, bank_, p0, p1, f1, extra_writes, tag, f0=0):
            small = (p1 - p0) * (f1 - f0) < 65536
            if small:
                eng = "V"
            else:
                ev_rr[0] ^= 1
                eng = "V" if ev_rr[0] else "A"
            if eng == "V":
                P.emit("V", lambda e: e.tensor_copy(dst_ap_fn(), bank_.v(p0, p1, f0, f1)),
                       reads=(bank_,), writes=extra_writes, tag=tag, special=small)
            else:
                P.emit("A", lambda e: e.copy(dst_ap_fn(), bank_.v(p0, p1, f0, f1)),
                       reads=(bank_,), writes=extra_writes, tag=tag, special=small)

        def tap(name, slot, p, f):
            if not debug:
                return
            dr = nc.dram_tensor("tap_" + name, [p, f], F32, kind="ExternalOutput").ap()
            tap_drams[name] = dr
            P.emit("G", lambda e, s=slot, dr=dr: e.dma_start(out=dr, in_=s.v(0, p, 0, f)),
                   reads=(slot,), writes=(), dma=True, tag="tap")

        xb = [pb.alloc(128, T, f"xb{c}") for c in range(NDC)]
        for c in range(NDC):
            ld(xb[c], d_xT[128 * c:128 * c + 128, :], f"ld_xb{c}")

        wkv = [pb.alloc(128, KVR, f"wkv{c}") for c in range(NDC)]
        wq = [pb.alloc(128, QR, f"wq{c}") for c in range(NDC)]
        wkrope = [pb.alloc(128, QR, f"wkr{c}") for c in range(NDC)]
        wqrope = [pb.alloc(128, QR, f"wqr{c}") for c in range(4)]
        wkdec = [pb.alloc(128, D, f"wkdec{c}") for c in range(2)]
        wv2 = [pb.alloc(128, NH * 65, f"wv2{c}") for c in range(2)]
        wqdec = [pb.alloc(128, D, f"wqdec{c}") for c in range(4)]
        cosd = pb.alloc(128, T, "cosd")
        sind = pb.alloc(128, T, "sind")
        dmaskb = pb.alloc(128, 128, "dmaskb")
        for c in range(NDC):
            ld(wkv[c], d_wkv[128 * c:128 * c + 128, :])
        for c in range(NDC):
            ld(wq[c], d_wq[128 * c:128 * c + 128, :])
        for c in range(NDC):
            ld(wkrope[c], d_wkrope[128 * c:128 * c + 128, :])
        for c in range(4):
            ld(wqrope[c], d_wqrope[128 * c:128 * c + 128, :])
        for c in range(2):
            ld(wkdec[c], d_wkdec[128 * c:128 * c + 128, :])
            ld(wv2[c], d_wv2[128 * c:128 * c + 128, :])
        for c in range(4):
            ld(wqdec[c], d_wqdec[128 * c:128 * c + 128, :])
        ld(cosd, d_cosd)
        ld(sind, d_sind)
        ld(dmaskb, d_dmask)

        identf = pf.alloc(128, 128, "identf")
        gkvf = pf.alloc(128, 2, "gkvf")
        ldY(identf, d_ident)
        P.emit("Y", lambda e: e.dma_start(out=gkvf.v(0, 128, 0, 1), in_=d_gkv[0:128, :]),
               writes=(gkvf,), dma=True)
        P.emit("Y", lambda e: e.dma_start(out=gkvf.v(0, 128, 1, 2), in_=d_gkv[128:256, :]),
               writes=(gkvf,), dma=True)

        onesb = pb.alloc(1, 128, "onesb")
        ones512 = pb.alloc(1, 512, "ones512")
        oc1024 = pb.alloc(128, 1, "oc1024")
        oc256 = pb.alloc(128, 1, "oc256")
        oc96 = pb.alloc(128, 1, "oc96")
        P.emit("V", lambda e: e.memset(onesb.ap, 1.0), writes=(onesb,), special=True)
        P.emit("V", lambda e: e.memset(ones512.ap, 1.0), writes=(ones512,), special=True)
        P.emit("V", lambda e: e.memset(oc1024.ap, 1.0 / 1024.0), writes=(oc1024,), special=True)
        P.emit("V", lambda e: e.memset(oc256.ap, 1.0 / 256.0), writes=(oc256,), special=True)
        P.emit("V", lambda e: e.memset(oc96.ap, 1.0), writes=(oc96,), special=True)

        for c in range(2):
            P.emit("V", lambda e, c=c: e.tensor_scalar(
                wkdec[c].ap, wkdec[c].ap, gkvf.v(0, 128, c, c + 1), None, ALU.mult),
                reads=(wkdec[c], gkvf), writes=(wkdec[c],), tag="foldg")
            P.emit("V", lambda e, c=c: e.tensor_scalar(
                wv2[c].ap, wv2[c].ap, gkvf.v(0, 128, c, c + 1), None, ALU.mult),
                reads=(wv2[c], gkvf), writes=(wv2[c],), tag="foldg")
        for c in range(4):
            P.emit("V", lambda e, c=c: e.tensor_scalar(
                wqdec[c].ap, wqdec[c].ap, SCALE, None, ALU.mult),
                reads=(wqdec[c],), writes=(wqdec[c],), tag="folds")

        tap("wkdec0", wkdec[0], 128, D)
        tap("wqdec0", wqdec[0], 128, D)
        tap("wkv0", wkv[0], 128, KVR)
        rowA = pf.alloc(1, T, "rowA")
        rowB = pf.alloc(1, T, "rowB")
        rowC = pf.alloc(1, T, "rowC")
        onesf = pf.alloc(1, 128, "onesf")
        P.emit("V", lambda e: e.memset(onesf.ap, 1.0), writes=(onesf,), special=True)
        epsc = pf.alloc(128, 3, "epsc")
        P.emit("V", lambda e: e.memset(epsc.v(0, 128, 0, 1), 1e-5), writes=(epsc,), special=True)
        P.emit("V", lambda e: e.memset(epsc.v(0, 128, 1, 2), 1e-6), writes=(epsc,), special=True)
        P.emit("V", lambda e: e.memset(epsc.v(0, 128, 2, 3), LN8H), writes=(epsc,), special=True)
        sqstat = [pb.alloc(128, T, f"sqstat{i}") for i in range(2)]

        def ln_stats(chunks, inv_col, eps_col, Rb, MRb, tag):
            nk = len(chunks)
            for c in range(nk):
                sq = sqstat[c % 2]
                P.emit("V", lambda e, c=c, sq=sq: e.tensor_tensor(
                    sq.ap, chunks[c].ap, chunks[c].ap, ALU.mult),
                    reads=(chunks[c],), writes=(sq,), tag=tag + "sq")
                for n in range(4):
                    mm(banks[n], lambda n=n: banks[n].v(0, 1, 0, 512),
                       lambda: inv_col.v(0, 128),
                       lambda c=c, n=n: chunks[c].v(0, 128, 512 * n, 512 * n + 512),
                       start=(c == 0), stop=(c == nk - 1),
                       reads=(inv_col, chunks[c]), tag=tag + "mean")
                    mm(banks[4 + n], lambda n=n: banks[4 + n].v(0, 1, 0, 512),
                       lambda: inv_col.v(0, 128),
                       lambda sq=sq, n=n: sq.v(0, 128, 512 * n, 512 * n + 512),
                       start=(c == 0), stop=(c == nk - 1),
                       reads=(inv_col, sq), tag=tag + "msq")
            for n in range(4):
                ev(lambda n=n: rowA.v(0, 1, 512 * n, 512 * n + 512), banks[n],
                   0, 1, 512, (rowA,), tag + "mev")
                ev(lambda n=n: rowB.v(0, 1, 512 * n, 512 * n + 512), banks[4 + n],
                   0, 1, 512, (rowB,), tag + "sev")
            P.emit("V", lambda e: e.tensor_tensor(rowC.ap, rowA.ap, rowA.ap, ALU.mult),
                   reads=(rowA,), writes=(rowC,), tag=tag + "m2", special=True)
            P.emit("V", lambda e: e.tensor_tensor(rowB.ap, rowB.ap, rowC.ap, ALU.subtract),
                   reads=(rowB, rowC), writes=(rowB,), tag=tag + "var", special=True)
            P.emit("A", lambda e: e.activation(rowB.ap, rowB.ap, AF.Ln,
                                               bias=epsc.v(0, 1, eps_col, eps_col + 1)),
                   reads=(rowB, epsc), writes=(rowB,), tag=tag + "ln", special=True)
            P.emit("A", lambda e: e.activation(rowB.ap, rowB.ap, AF.Exp, scale=-0.5),
                   reads=(rowB,), writes=(rowB,), tag=tag + "exp", special=True)
            P.emit("V", lambda e: e.tensor_tensor(rowC.ap, rowA.ap, rowB.ap, ALU.mult),
                   reads=(rowA, rowB), writes=(rowC,), tag=tag + "mr", special=True)
            for n in range(4):
                mm(banks[n % 2], lambda n=n: banks[n % 2].v(0, 128, 0, 512),
                   lambda: onesf.v(0, 1, 0, 128),
                   lambda n=n: rowB.v(0, 1, 512 * n, 512 * n + 512),
                   start=True, stop=True, reads=(onesf, rowB), tag=tag + "rbc")
                ev(lambda n=n: Rb.v(0, 128, 512 * n, 512 * n + 512), banks[n % 2],
                   0, 128, 512, (Rb,), tag + "rbe")
                mm(banks[2 + n % 2], lambda n=n: banks[2 + n % 2].v(0, 128, 0, 512),
                   lambda: onesf.v(0, 1, 0, 128),
                   lambda n=n: rowC.v(0, 1, 512 * n, 512 * n + 512),
                   start=True, stop=True, reads=(onesf, rowC), tag=tag + "mbc")
                ev(lambda n=n: MRb.v(0, 128, 512 * n, 512 * n + 512), banks[2 + n % 2],
                   0, 128, 512, (MRb,), tag + "mbe")

        R1b = pb.alloc(128, T, "R1b")
        M1b = pb.alloc(128, T, "M1b")
        ln_stats(xb, oc1024, 0, R1b, M1b, "ln1")
        for c in range(NDC):
            P.emit("V", lambda e, c=c: e.tensor_tensor(
                xb[c].ap, xb[c].ap, R1b.ap, ALU.mult),
                reads=(xb[c], R1b), writes=(xb[c],), tag="ln1m")
            P.emit("V", lambda e, c=c: e.tensor_tensor(
                xb[c].ap, xb[c].ap, M1b.ap, ALU.subtract),
                reads=(xb[c], M1b), writes=(xb[c],), tag="ln1s")
        tap("h1c0", xb[0], 128, T)
        pb.free(R1b, M1b)
        P.barrier()
        pb.reclaim()

        kvn = [pb.alloc(128, T, f"kvn{c}") for c in range(2)]
        qc_ = [pb.alloc(128, T, f"qc{c}") for c in range(4)]
        krot = [pb.alloc(128, T, f"krot{c}") for c in range(4)]
        qrot = [pb.alloc(128, T, f"qrot{c}") for c in range(4)]
        bank_rr = [0]

        def next_bank():
            bank_rr[0] = (bank_rr[0] + 1) % 8
            return banks[bank_rr[0]]

        def proj(dst, wch, src, nk, mlo, tag):
            for n in range(4):
                b_ = next_bank()
                for k in range(nk):
                    mm(b_, lambda b_=b_: b_.v(0, 128, 0, 512),
                       lambda k=k: wch[k].v(0, 128, mlo, mlo + 128),
                       lambda k=k, n=n: src[k].v(0, 128, 512 * n, 512 * n + 512),
                       start=(k == 0), stop=(k == nk - 1),
                       reads=(wch[k], src[k]), tag=tag)
                ev(lambda n=n, dst=dst: dst.v(0, 128, 512 * n, 512 * n + 512), b_,
                   0, 128, 512, (dst,), tag + "e")

        for m in range(2):
            proj(kvn[m], wkv, xb, NDC, 128 * m, "kv")
        for m in range(4):
            proj(qc_[m], wq, xb, NDC, 128 * m, "qc")
        for m in range(4):
            proj(krot[m], wkrope, xb, NDC, 128 * m, "kraw")
        for m in range(4):
            proj(qrot[m], wqrope, qc_, 4, 128 * m, "qraw")

        for c in range(2):
            sq = sqstat[c % 2]
            P.emit("V", lambda e, c=c, sq=sq: e.tensor_tensor(
                sq.ap, kvn[c].ap, kvn[c].ap, ALU.mult),
                reads=(kvn[c],), writes=(sq,), tag="rmssq")
            for n in range(4):
                mm(banks[n], lambda n=n: banks[n].v(0, 1, 0, 512),
                   lambda: oc256.v(0, 128),
                   lambda sq=sq, n=n: sq.v(0, 128, 512 * n, 512 * n + 512),
                   start=(c == 0), stop=(c == 1),
                   reads=(oc256, sq), tag="rmsmm")
        for n in range(4):
            ev(lambda n=n: rowA.v(0, 1, 512 * n, 512 * n + 512), banks[n],
               0, 1, 512, (rowA,), "rmsev")
        P.emit("A", lambda e: e.activation(rowA.ap, rowA.ap, AF.Ln,
                                           bias=epsc.v(0, 1, 1, 2)),
               reads=(rowA, epsc), writes=(rowA,), tag="rmsln", special=True)
        P.emit("A", lambda e: e.activation(rowA.ap, rowA.ap, AF.Exp, scale=-0.5),
               reads=(rowA,), writes=(rowA,), tag="rmsexp", special=True)
        RMSb = pb.alloc(128, T, "RMSb")
        for n in range(4):
            mm(banks[n % 2], lambda n=n: banks[n % 2].v(0, 128, 0, 512),
               lambda: onesf.v(0, 1, 0, 128),
               lambda n=n: rowA.v(0, 1, 512 * n, 512 * n + 512),
               start=True, stop=True, reads=(onesf, rowA), tag="rmsbc")
            ev(lambda n=n: RMSb.v(0, 128, 512 * n, 512 * n + 512), banks[n % 2],
               0, 128, 512, (RMSb,), "rmsbe")
        for c in range(2):
            P.emit("V", lambda e, c=c: e.tensor_tensor(
                kvn[c].ap, kvn[c].ap, RMSb.ap, ALU.mult),
                reads=(kvn[c], RMSb), writes=(kvn[c],), tag="rmsap")
        tap("kvnc0", kvn[0], 128, T)

        shmask = list(range(16, 32)) + list(range(0, 16))
        rscr = [(sqstat[0], R1b), (sqstat[1], M1b)]
        rscr = [(sqstat[0], sqstat[1])]
        for gi, group in enumerate((krot, qrot)):
            for c in range(4):
                s = group[c]
                rs1, rs2 = rscr[0]
                if (gi * 4 + c) % 2:
                    rs2, rs1 = rscr[0]
                P.emit("V", lambda e, s=s, rs1=rs1: e.stream_shuffle(rs1.ap, s.ap, shmask),
                       reads=(s,), writes=(rs1,), tag="rpsh")
                P.emit("V", lambda e, s=s, rs2=rs2: e.tensor_tensor(
                    rs2.ap, s.ap, cosd.ap, ALU.mult),
                    reads=(s, cosd), writes=(rs2,), tag="rpc")
                P.emit("V", lambda e, rs1=rs1: e.tensor_tensor(
                    rs1.ap, rs1.ap, sind.ap, ALU.mult),
                    reads=(rs1, sind), writes=(rs1,), tag="rps")
                P.emit("V", lambda e, s=s, rs1=rs1, rs2=rs2: e.tensor_tensor(
                    s.ap, rs2.ap, rs1.ap, ALU.add),
                    reads=(rs1, rs2), writes=(s,), tag="rpa")
        tap("krotc0", krot[0], 128, T)
        tap("qrotc0", qrot[0], 128, T)

        pb.free(*xb, *wkv, *wq, *wkrope, *wqrope, cosd, sind, RMSb, *sqstat)
        P.barrier()
        pb.reclaim()
        pf.reclaim()

        sqs = pb.alloc(128, T, "sqs")
        sqsc = [Slot(AR_B, sqs.off + 512 * n, 128, 512, f"sqc{n}") for n in range(4)]
        vtok_base = pb.alloc(128, NH * 65 * NT, "vtok")
        vtok = [Slot(AR_B, vtok_base.off + NH * 65 * i, 128, NH * 65, f"vt{i}")
                for i in range(NT)]

        def vtok_thunk(i):
            def t():
                for n in range(3):
                    ncols = (512, 512, 16)[n]
                    nlo = 512 * n
                    b_ = asm_bank()
                    for k in range(2):
                        mm(b_, lambda b_=b_, ncols=ncols: b_.v(0, 128, 0, ncols),
                           lambda k=k, i=i: kvn[k].v(0, 128, 128 * i, 128 * i + 128),
                           lambda k=k, nlo=nlo, ncols=ncols: wv2[k].v(0, 128, nlo, nlo + ncols),
                           start=(k == 0), stop=(k == 1),
                           reads=(kvn[k], wv2[k]), tag="vdec")
                    ev(lambda i=i, nlo=nlo, ncols=ncols: vtok[i].v(0, 128, nlo, nlo + ncols),
                       b_, 0, 128, ncols, (vtok[i],), "vdece")
                P.emit("V", lambda e, i=i: e.memset(
                    vtok[i].ap.rearrange("p (h c) -> p h c", h=NH)[:, :, 64:65], 1.0),
                    writes=(vtok[i],), tag="vones", special=True)
            return t


        attnT = [pb.alloc(128, T, f"aT{c}") for c in range(NDC)]
        rK = [pb.alloc(97, T, "rK0"), pb.alloc(97, T, "rK1")]
        rQ = [pb.alloc(97, T, "rQ0"), pb.alloc(97, T, "rQ1")]
        U = [pb.alloc(128, 1024, f"U{i}") for i in range(3)]
        rzf = [pf.alloc(1, 512, "rzf0"), pf.alloc(1, 512, "rzf1")]
        rzb = [pb.alloc(1, 512, "rzb0"), pb.alloc(1, 512, "rzb1")]
        rz_rr = [0]
        RBt = [pb.alloc(64, 512, "RBt0"), pb.alloc(64, 512, "RBt1")]
        nmk2 = pf.alloc(1, 4, "nmk2")
        asm_rr = [0]
        avsets = [(banks[4], banks[5]), (banks[6], banks[7])]
        borrow = [1]
        can_asm = [True]
        g_cur = [0]

        def asm_bank():
            asm_rr[0] ^= 1
            return avsets[borrow[0]][asm_rr[0]]

        def assemble_head(hh):
            Kd, Qd = rK[hh % 2], rQ[hh % 2]
            th = []
            tks, tqs = [], []
            for n in range(4):
                def t_k(n=n, hh=hh, Kd=Kd):
                    b_ = asm_bank()
                    for k in range(2):
                        mm(b_, lambda b_=b_: b_.v(0, 64, 0, 512),
                           lambda k=k, hh=hh: wkdec[k].v(0, 128, 64 * hh, 64 * hh + 64),
                           lambda k=k, n=n: kvn[k].v(0, 128, 512 * n, 512 * n + 512),
                           start=(k == 0), stop=(k == 1),
                           reads=(wkdec[k], kvn[k]), tag="kdec")
                    ev(lambda n=n, Kd=Kd: Kd.v(0, 64, 512 * n, 512 * n + 512), b_,
                       0, 64, 512, (Kd,), "kdeA")
                tks.append(t_k)
                def t_q(n=n, hh=hh, Qd=Qd):
                    b_ = asm_bank()
                    for k in range(4):
                        mm(b_, lambda b_=b_: b_.v(0, 64, 0, 512),
                           lambda k=k, hh=hh: wqdec[k].v(0, 128, 64 * hh, 64 * hh + 64),
                           lambda k=k, n=n: qc_[k].v(0, 128, 512 * n, 512 * n + 512),
                           start=(k == 0), stop=(k == 3),
                           reads=(wqdec[k], qc_[k]), tag="qdec")
                    ev(lambda n=n, Qd=Qd: Qd.v(0, 64, 512 * n, 512 * n + 512), b_,
                       0, 64, 512, (Qd,), "qdeA")
                tqs.append(t_q)
            cidx, rlo = hh // 4, 32 * (hh % 4)
            def t_rope(cidx=cidx, rlo=rlo, Kd=Kd, Qd=Qd):
                P.emit("V", lambda e: e.tensor_copy(
                    Kd.v(64, 96), krot[cidx].v(rlo, rlo + 32)),
                    reads=(krot[cidx],), writes=(Kd,), tag="rcpK")
                P.emit("V", lambda e: e.tensor_scalar(
                    Qd.v(64, 96), qrot[cidx].v(rlo, rlo + 32), SCALE, None,
                    ALU.mult), reads=(qrot[cidx],), writes=(Qd,), tag="rcpQ")
                P.emit("V", lambda e: e.memset(Kd.v(96, 97), 1.0),
                       writes=(Kd,), tag="kones", special=True)
            def t_ksq(Kd=Kd):
                for n in range(4):
                    P.emit("V", lambda e, n=n: e.tensor_tensor(
                        sqsc[n].v(0, 96), Kd.v(0, 96, 512 * n, 512 * n + 512),
                        Kd.v(0, 96, 512 * n, 512 * n + 512), ALU.mult),
                        reads=(Kd,), writes=(sqsc[n],), tag="ksq")
            def sum2(nlo, rcv, tag):
                bb = [asm_bank(), asm_bank()]
                for i in range(2):
                    mm(bb[i], lambda b_=bb[i]: b_.v(0, 1, 0, 512),
                       lambda: oc96.v(0, 96),
                       lambda n=nlo + i: sqsc[n].v(0, 96),
                       start=True, stop=True, reads=(oc96, sqsc[nlo + i]), tag=tag)
                for i in range(2):
                    rcv(nlo + i, bb[i])
            def rcv_k(n, b_):
                ev(lambda n=n: rowA.v(0, 1, 512 * n, 512 * n + 512), b_,
                   0, 1, 512, (rowA,), "kn2e")
            def t_kn2a():
                sum2(0, rcv_k, "kn2")
            def t_kn2b():
                sum2(2, rcv_k, "kn2")
                P.emit("V", lambda e: e.reduce_max(
                    nmk2.v(0, 1, 0, 1), rowA.ap, axis=AX.X),
                    reads=(rowA,), writes=(nmk2,), tag="kn2m", special=True)
                P.emit("A", lambda e: e.activation(
                    nmk2.v(0, 1, 1, 2), nmk2.v(0, 1, 0, 1), AF.Ln),
                    reads=(nmk2,), writes=(nmk2,), tag="mkln", special=True)
                P.emit("A", lambda e: e.activation(
                    nmk2.v(0, 1, 2, 3), nmk2.v(0, 1, 1, 2), AF.Exp, scale=0.5),
                    reads=(nmk2,), writes=(nmk2,), tag="mkexp", special=True)
                P.emit("V", lambda e: e.tensor_scalar(
                    nmk2.v(0, 1, 2, 3), nmk2.v(0, 1, 2, 3), -0.5, None, ALU.mult),
                    reads=(nmk2,), writes=(nmk2,), tag="mkneg", special=True)
            def t_qsq(Qd=Qd):
                for n in range(4):
                    P.emit("V", lambda e, n=n: e.tensor_tensor(
                        sqsc[n].v(0, 96), Qd.v(0, 96, 512 * n, 512 * n + 512),
                        Qd.v(0, 96, 512 * n, 512 * n + 512), ALU.mult),
                        reads=(Qd,), writes=(sqsc[n],), tag="qsq")
            def rcv_q(n, b_, Qd=Qd):
                P.emit("V", lambda e, n=n, b_=b_: e.tensor_scalar(
                    Qd.v(96, 97, 512 * n, 512 * n + 512), b_.v(0, 1, 0, 512),
                    1.0, nmk2.v(0, 1, 2, 3), ALU.add, ALU.mult),
                    reads=(b_, nmk2), writes=(Qd,), tag="bqr", special=True)
            def t_qn2a():
                sum2(0, rcv_q, "qn2")
            def t_qn2b():
                sum2(2, rcv_q, "qn2")
            def mk_kq(a, b):
                def t():
                    a()
                    b()
                return t
            th.extend([mk_kq(a, b) for a, b in zip(tks, tqs)])
            th.extend([t_rope, t_ksq, None, None, t_kn2a, t_kn2b,
                       t_qsq, None, t_qn2a, t_qn2b])
            return th

        deferq = []

        def av_evac(hh, qc, bank_, flip=None):
            r = rz_rr[0]
            rz_rr[0] = (r + 1) % 2
            zf, zb = rzf[r], rzb[r]
            P.emit("A", lambda e, bank_=bank_, zf=zf: e.activation(
                zf.ap, bank_.v(64, 65, 0, 512), AF.Ln),
                reads=(bank_,), writes=(zf,), tag="rz", special=True)
            P.emit("A", lambda e, zf=zf, zb=zb: e.activation(
                zb.ap, zf.ap, AF.Exp, scale=-1.0),
                reads=(zf,), writes=(zb,), tag="rzc", special=True)
            deferq.append((g_cur[0] + 3,
                           lambda: av_evac_now(hh, qc, bank_, zb, r, flip)))

        def av_evac_now(hh, qc, bank_, zb, r, flip=None):
            Rb = RBt[r]
            mm(bank_, lambda bank_=bank_: bank_.v(64, 128, 0, 512),
               lambda: onesb.v(0, 1, 0, 64), lambda zb=zb: zb.ap,
               start=True, stop=True, reads=(onesb, zb), tag="rzbc")
            ev(lambda Rb=Rb: Rb.ap, bank_, 64, 128, 512, (Rb,), "rzbe")
            c, rlo = hh // 2, 64 * (hh % 2)
            P.emit("V", lambda e, bank_=bank_, c=c, rlo=rlo, qc=qc, Rb=Rb:
                   e.tensor_tensor(
                       attnT[c].v(rlo, rlo + 64, 512 * qc, 512 * qc + 512),
                       bank_.v(0, 64, 0, 512), Rb.ap, ALU.mult),
                   reads=(bank_, Rb), writes=(attnT[c],), tag="avn")
            if flip is not None:
                borrow[0] = flip
                can_asm[0] = True

        def pump(k=1):
            for _ in range(k):
                if deferq and g_cur[0] >= deferq[0][0]:
                    deferq.pop(0)[1]()
                elif co and can_asm[0]:
                    t = co.pop(0)
                    if t is not None:
                        t()

        def flush_evacs():
            while deferq:
                deferq.pop(0)[1]()

        def emit_S(hh, qlo, j, gpar, uslot):
            Kh, Qh = rK[hh % 2], rQ[hh % 2]
            qs0 = 128 * j
            glo = 512 * qlo
            spanT = span_t[gpar]
            sbanks = (banks[2 * gpar], banks[2 * gpar + 1])
            sc = max(glo, qs0)
            for qc in (qlo, qlo + 1):
                cl, ch = 512 * qc, 512 * qc + 512
                if ch <= qs0:
                    continue
                scl = max(cl, qs0)
                mm(sbanks[qc - qlo],
                   lambda spanT=spanT, scl=scl, ch=ch, glo=glo:
                   spanT[0:128, scl - glo:ch - glo],
                   lambda Kh=Kh, j=j: Kh.v(0, 97, 128 * j, 128 * j + 128),
                   lambda Qh=Qh, scl=scl, ch=ch: Qh.v(0, 97, scl, ch),
                   start=True, stop=True, reads=(Kh, Qh), tag="S")
            u = U[uslot]
            P.emit("A", lambda e, spanT=spanT, sc=sc, glo=glo, u=u:
                   e.activation(u.v(0, 128, sc - glo, 1024),
                                spanT[0:128, sc - glo:1024], AF.Exp),
                   reads=list(sbanks), writes=(u,), tag="exp")
            if glo <= qs0:
                q0 = qs0 - glo
                P.emit("G", lambda e, u=u, q0=q0: e.tensor_tensor(
                    u.v(0, 128, q0, q0 + 128), u.v(0, 128, q0, q0 + 128),
                    dmaskb.ap, ALU.mult),
                    reads=(u, dmaskb), writes=(u,), tag="dmk")

        def emit_AV(hh, qlo, j, pidx, uslot):
            qs0 = 128 * j
            u = U[uslot]
            for qc in (qlo, qlo + 1):
                cl, ch = 512 * qc, 512 * qc + 512
                if ch <= qs0:
                    continue
                scl = max(cl, qs0)
                b_ = avsets[pidx % 2][qc - qlo]
                glo = 512 * qlo
                if j == 0 and qc == qlo and borrow[0] == pidx % 2:
                    can_asm[0] = False
                mm(b_, lambda b_=b_, scl=scl, cl=cl, ch=ch:
                   b_.v(0, 65, scl - cl, ch - cl),
                   lambda j=j, hh=hh: vtok[j].v(0, 128, 65 * hh, 65 * hh + 65),
                   lambda u=u, scl=scl, ch=ch, glo=glo: u.v(0, 128, scl - glo, ch - glo),
                   start=(j == 0), stop=(j == 4 * qc + 3),
                   reads=(vtok[j], u), tag="AV")
                if j == 4 * qc + 3:
                    av_evac(hh, qc, b_,
                            flip=(pidx % 2) if qc == qlo + 1 else None)

        for i in range(NT):
            vtok_thunk(i)()
        for t in assemble_head(0):
            if t is not None:
                t()
        tap("K0", rK[0], 97, T)
        tap("Q0", rQ[0], 97, T)
        stream = []
        for hh in range(NH):
            for (qlo, jmax) in ((0, 8), (2, 16)):
                pidx = 2 * hh + (1 if qlo else 0)
                for j in range(jmax):
                    stream.append((hh, qlo, j, pidx))
        pend = []
        co = []
        cur_h = -1
        for g, (hh, qlo, j, pidx) in enumerate(stream):
            g_cur[0] = g
            if hh != cur_h:
                while co:
                    t = co.pop(0)
                    if t is not None:
                        t()
                cur_h = hh
                co = list(assemble_head(hh + 1)) if hh < NH - 1 else []
            emit_S(hh, qlo, j, g % 2, g % 3)
            pend.append((hh, qlo, j, pidx, g % 3))
            if len(pend) > 2:
                emit_AV(*pend.pop(0))
            pump(1)
        g_cur[0] += 10
        while pend:
            emit_AV(*pend.pop(0))
        flush_evacs()
        tap("attnT0", attnT[0], 128, T)

        pb.free(*rK, *rQ, *U, vtok_base, *kvn, *qc_, *krot, *qrot, *wkdec, *wqdec,
                *rzb, *RBt, sqs, *wv2)
        pf.free(*rzf, nmk2)
        P.barrier()
        pb.reclaim()
        pf.reclaim()

        wo = [pb.alloc(128, D, f"wo{c}") for c in range(NDC)]
        for c in range(NDC):
            ld(wo[c], d_wo[128 * c:128 * c + 128, :], "ld_wo")
        attnP = [pb.alloc(128, T, f"aP{m}") for m in range(NDC)]
        for m in range(NDC):
            for n in range(4):
                b_ = next_bank()
                for k in range(NDC):
                    mm(b_, lambda b_=b_: b_.v(0, 128, 0, 512),
                       lambda k=k, m=m: wo[k].v(0, 128, 128 * m, 128 * m + 128),
                       lambda k=k, n=n: attnT[k].v(0, 128, 512 * n, 512 * n + 512),
                       start=(k == 0), stop=(k == NDC - 1),
                       reads=(wo[k], attnT[k]), tag="woMM")
                ev(lambda m=m, n=n: attnP[m].v(0, 128, 512 * n, 512 * n + 512), b_,
                   0, 128, 512, (attnP[m],), "woev")
        tap("attnP0", attnP[0], 128, T)

        pb.free(*attnT, *wo)
        P.barrier()
        pb.reclaim()

        wvT = [pb.alloc(128, D, f"wvT{c}") for c in range(NDC)]
        gaterhs = [pb.alloc(128, 72, f"gr{c}") for c in range(NDC)]
        w1catT = [pb.alloc(128, E_ * INNER, f"w1{c}") for c in range(NDC)]
        w2cat = [pb.alloc(128, D, f"w2{c}") for c in range(2)]
        b2b = pb.alloc(E_, D, "b2b")
        bvrow = pb.alloc(1, D, "bvrow")
        b1row = pb.alloc(1, E_ * INNER, "b1row")
        bg72row = pb.alloc(1, 72, "bg72")
        selb = pb.alloc(E_, 256, "selb")
        weT = pb.alloc(E_, T, "weT")
        weTc = [Slot(AR_B, weT.off + 512 * n, E_, 512, f"weTc{n}") for n in range(4)]
        grcbf = pf.alloc(128, 32, "grcbf")
        sigcol = pf.alloc(128, 2, "sigcol")
        xfull = [pb.alloc(128, T, f"xf{c}") for c in range(NDC)]
        for c in range(NDC):
            ld(xfull[c], d_xT[128 * c:128 * c + 128, :], "ld_x2")
        ld(b2b, d_b2, "ld_b2")
        ld(bvrow, d_bv, "ld_bv")
        ld(b1row, d_b1row, "ld_b1")
        ld(bg72row, d_bg72, "ld_bg")
        ld(selb, d_sel, "ld_sel")
        for c in range(NDC):
            ld(wvT[c], d_wvT[128 * c:128 * c + 128, :], "ld_wvT")
        for c in range(NDC):
            ld(gaterhs[c], d_gaterhs[128 * c:128 * c + 128, :], "ld_gr")
        for c in range(NDC):
            ld(w1catT[c], d_w1catT[128 * c:128 * c + 128, :], "ld_w1")
        for c in range(2):
            ld(w2cat[c], d_w2cat[128 * c:128 * c + 128, :], "ld_w2")
        ldY(grcbf, d_grcb, "ld_grcb")
        P.emit("Y", lambda e: e.dma_start(out=sigcol.v(0, 128, 0, 1), in_=d_alpha),
               writes=(sigcol,), dma=True, tag="ld_alpha")
        P.emit("A", lambda e: e.activation(
            sigcol.v(0, 128, 1, 2), sigcol.v(0, 128, 0, 1), AF.Sigmoid),
            reads=(sigcol,), writes=(sigcol,), tag="sig", special=True)

        sqstat[0] = pb.alloc(128, T, "sqstat0b")
        sqstat[1] = pb.alloc(128, T, "sqstat1b")
        R2b = pb.alloc(128, T, "R2b")
        M2b = pb.alloc(128, T, "M2b")
        ln_stats(attnP, oc1024, 0, R2b, M2b, "ln2")
        x2 = [pb.alloc(128, T, f"x2{c}") for c in range(NDC)]
        for c in range(NDC):
            xr = xfull[c]
            tscr = sqstat[c % 2]
            P.emit("V", lambda e, c=c, tscr=tscr: e.tensor_tensor(
                tscr.ap, attnP[c].ap, R2b.ap, ALU.mult),
                reads=(attnP[c], R2b), writes=(tscr,), tag="x2a")
            P.emit("V", lambda e, tscr=tscr: e.tensor_tensor(
                tscr.ap, tscr.ap, M2b.ap, ALU.subtract),
                reads=(tscr, M2b), writes=(tscr,), tag="x2b")
            P.emit("V", lambda e, c=c, xr=xr, tscr=tscr: e.tensor_tensor(
                x2[c].ap, tscr.ap, xr.ap, ALU.add),
                reads=(tscr, xr), writes=(x2[c],), tag="x2c")
        tap("x2c0", x2[0], 128, T)

        pb.free(*attnP, R2b, M2b, *xfull)
        P.barrier()
        pb.reclaim()
        pf.reclaim()

        R3b = pb.alloc(128, T, "R3b")
        M3b = pb.alloc(128, T, "M3b")
        ln_stats(x2, oc1024, 0, R3b, M3b, "ln3")
        h2 = [pb.alloc(128, T, f"h2{c}") for c in range(NDC)]
        for c in range(NDC):
            P.emit("V", lambda e, c=c: e.tensor_tensor(
                h2[c].ap, x2[c].ap, R3b.ap, ALU.mult),
                reads=(x2[c], R3b), writes=(h2[c],), tag="h2a")
            P.emit("V", lambda e, c=c: e.tensor_tensor(
                h2[c].ap, h2[c].ap, M3b.ap, ALU.subtract),
                reads=(h2[c], M3b), writes=(h2[c],), tag="h2b")
        pb.free(R3b, M3b)
        tap("h2c0", h2[0], 128, T)

        vd = [pb.alloc(128, T, f"vd{m}") for m in range(NDC)]
        for m in range(NDC):
            for n in range(4):
                b_ = next_bank()
                for k in range(NDC):
                    mm(b_, lambda b_=b_: b_.v(0, 128, 0, 512),
                       lambda k=k, m=m: wvT[k].v(0, 128, 128 * m, 128 * m + 128),
                       lambda k=k, n=n: h2[k].v(0, 128, 512 * n, 512 * n + 512),
                       start=(k == 0), stop=False, reads=(wvT[k], h2[k]), tag="vdMM")
                mm(b_, lambda b_=b_: b_.v(0, 128, 0, 512),
                   lambda m=m: bvrow.v(0, 1, 128 * m, 128 * m + 128),
                   lambda: ones512.ap,
                   start=False, stop=True, reads=(bvrow, ones512), tag="vdAug")
                ev(lambda m=m, n=n: vd[m].v(0, 128, 512 * n, 512 * n + 512), b_,
                   0, 128, 512, (vd[m],), "vdev")
        tap("vdc0", vd[0], 128, T)
        pb.free(*h2, *wvT)
        P.barrier()
        pb.reclaim()

        P.hard_all = True
        gbuf = []
        for s_ in range(4):
            gbuf.append(dict(
                gt=pf.alloc(128, 72, f"gt{s_}"),
                g64=pf.alloc(128, 64, f"g64{s_}"),
                g32=pf.alloc(128, 32, f"g32{s_}"),
                g16=pf.alloc(128, 16, f"g16{s_}"),
                g8a=pf.alloc(128, 8, f"g8a{s_}"),
                g8b=pf.alloc(128, 8, f"g8b{s_}"),
                g8c=pf.alloc(128, 8, f"g8c{s_}"),
                gcol=pf.alloc(128, 8, f"gcol{s_}"),
            ))
        web = pb.alloc(128, 8, "web")

        def v3s(slot, e_, q0, q1):
            return slot.ap.rearrange("p (e q) -> p e q", e=e_)[:, :, q0:q1]

        def gating_thunks(i, B):
            gt, g64, g32, g16 = B["gt"], B["g64"], B["g32"], B["g16"]
            g8a, g8b, g8c, gcol = B["g8a"], B["g8b"], B["g8c"], B["gcol"]
            th = []
            b_ = next_bank()
            def t_mm(b_=b_, i=i):
                for k in range(NDC):
                    mm(b_, lambda b_=b_: b_.v(0, 128, 0, 72),
                       lambda k=k, i=i: vd[k].v(0, 128, 128 * i, 128 * i + 128),
                       lambda k=k: gaterhs[k].ap,
                       start=(k == 0), stop=False, reads=(vd[k], gaterhs[k]), tag="gMM")
                mm(b_, lambda b_=b_: b_.v(0, 128, 0, 72),
                   lambda: ones512.v(0, 1, 0, 128),
                   lambda: bg72row.ap,
                   start=False, stop=True, reads=(ones512, bg72row), tag="gAug")
                P.emit("V", lambda e, b_=b_: e.tensor_copy(gt.ap, b_.v(0, 128, 0, 72)),
                       reads=(b_,), writes=(gt,), tag="gev")
            th.append(t_mm)
            th.append(lambda: P.emit("V", lambda e: e.tensor_tensor(
                g64.ap, gt.v(0, 128, 8, 72), gt.v(0, 128, 8, 72), ALU.mult),
                reads=(gt,), writes=(g64,), tag="gsq"))
            th.append(lambda: P.emit("V", lambda e: e.tensor_tensor(
                g32.ap.rearrange("p (e q) -> p e q", e=8),
                v3s(g64, 8, 0, 4), v3s(g64, 8, 4, 8), ALU.add),
                reads=(g64,), writes=(g32,), tag="gt0"))
            th.append(lambda: P.emit("V", lambda e: e.tensor_tensor(
                g16.ap.rearrange("p (e q) -> p e q", e=8),
                v3s(g32, 8, 0, 2), v3s(g32, 8, 2, 4), ALU.add),
                reads=(g32,), writes=(g16,), tag="gt1"))
            th.append(lambda: P.emit("V", lambda e: e.tensor_tensor(
                g8b.ap.rearrange("p (e q) -> p e q", e=8),
                v3s(g16, 8, 0, 1), v3s(g16, 8, 1, 2), ALU.add),
                reads=(g16,), writes=(g8b,), tag="gt2"))
            th.append(lambda: P.emit("A", lambda e: e.activation(g8b.ap, g8b.ap, AF.Ln),
                   reads=(g8b,), writes=(g8b,), tag="gln"))
            th.append(lambda: P.emit("A", lambda e: e.activation(
                g8b.ap, g8b.ap, AF.Exp, scale=0.5, bias=epsc.v(0, 128, 2, 3)),
                reads=(g8b, epsc), writes=(g8b,), tag="gexp"))

            def ln8(src_ap_fn, dst, res_in, tagp):
                th.append(lambda: P.emit("V", lambda e: e.tensor_reduce(
                    gcol.v(0, 128, 0, 1), src_ap_fn(), axis=AX.X, op=ALU.add),
                    reads=(res_in,), writes=(gcol,), tag=tagp + "sum"))
                th.append(lambda: P.emit("V", lambda e: e.tensor_scalar(
                    gcol.v(0, 128, 0, 1), gcol.v(0, 128, 0, 1), 0.125, None, ALU.mult),
                    reads=(gcol,), writes=(gcol,), tag=tagp + "mean"))
                th.append(lambda: P.emit("V", lambda e: e.tensor_tensor(
                    g8c.ap, src_ap_fn(), src_ap_fn(), ALU.mult),
                    reads=(res_in,), writes=(g8c,), tag=tagp + "sq"))
                th.append(lambda: P.emit("V", lambda e: e.tensor_reduce(
                    gcol.v(0, 128, 1, 2), g8c.ap, axis=AX.X, op=ALU.add),
                    reads=(g8c,), writes=(gcol,), tag=tagp + "sumsq"))
                th.append(lambda: P.emit("V", lambda e: e.tensor_scalar(
                    gcol.v(0, 128, 1, 2), gcol.v(0, 128, 1, 2), 0.125, None, ALU.mult),
                    reads=(gcol,), writes=(gcol,), tag=tagp + "msq"))
                th.append(lambda: P.emit("V", lambda e: e.tensor_tensor(
                    gcol.v(0, 128, 2, 3), gcol.v(0, 128, 0, 1), gcol.v(0, 128, 0, 1),
                    ALU.mult), reads=(gcol,), writes=(gcol,), tag=tagp + "m2"))
                th.append(lambda: P.emit("V", lambda e: e.tensor_tensor(
                    gcol.v(0, 128, 1, 2), gcol.v(0, 128, 1, 2), gcol.v(0, 128, 2, 3),
                    ALU.subtract), reads=(gcol,), writes=(gcol,), tag=tagp + "var"))
                th.append(lambda: P.emit("A", lambda e: e.activation(
                    gcol.v(0, 128, 1, 2), gcol.v(0, 128, 1, 2), AF.Ln,
                    bias=epsc.v(0, 128, 0, 1)),
                    reads=(gcol, epsc), writes=(gcol,), tag=tagp + "ln"))
                th.append(lambda: P.emit("A", lambda e: e.activation(
                    gcol.v(0, 128, 1, 2), gcol.v(0, 128, 1, 2), AF.Exp, scale=-0.5),
                    reads=(gcol,), writes=(gcol,), tag=tagp + "rstd"))
                th.append(lambda: P.emit("V", lambda e: e.tensor_scalar(
                    dst.ap, src_ap_fn(), gcol.v(0, 128, 0, 1), gcol.v(0, 128, 1, 2),
                    ALU.subtract, ALU.mult),
                    reads=(res_in, gcol), writes=(dst,), tag=tagp + "app"))
            ln8(lambda: gt.v(0, 128, 0, 8), g8a, gt, "lnr")
            ln8(lambda: g8b.ap, g8c, g8b, "lnc")
            th.append(lambda: P.emit("V", lambda e: e.tensor_tensor(
                g8a.ap, g8a.ap, grcbf.v(0, 128, 0, 8), ALU.mult),
                reads=(g8a, grcbf), writes=(g8a,), tag="sgr"))
            th.append(lambda: P.emit("V", lambda e: e.tensor_tensor(
                g8a.ap, g8a.ap, grcbf.v(0, 128, 8, 16), ALU.add),
                reads=(g8a, grcbf), writes=(g8a,), tag="sbr"))
            th.append(lambda: P.emit("V", lambda e: e.tensor_tensor(
                g8c.ap, g8c.ap, grcbf.v(0, 128, 16, 24), ALU.mult),
                reads=(g8c, grcbf), writes=(g8c,), tag="sgc"))
            th.append(lambda: P.emit("V", lambda e: e.tensor_tensor(
                g8c.ap, g8c.ap, grcbf.v(0, 128, 24, 32), ALU.add),
                reads=(g8c, grcbf), writes=(g8c,), tag="sbc"))
            th.append(lambda: P.emit("V", lambda e: e.tensor_scalar(
                g8c.ap, g8c.ap, sigcol.v(0, 128, 1, 2), None, ALU.mult),
                reads=(g8c, sigcol), writes=(g8c,), tag="ssig"))
            th.append(lambda: P.emit("V", lambda e: e.tensor_tensor(
                g8a.ap, g8a.ap, g8c.ap, ALU.add),
                reads=(g8a, g8c), writes=(g8a,), tag="ssum"))
            th.append(lambda: P.emit("V", lambda e: e.tensor_reduce(
                gcol.v(0, 128, 1, 2), g8a.ap, axis=AX.X, op=ALU.max),
                reads=(g8a,), writes=(gcol,), tag="m1"))
            th.append(lambda: P.emit("V", lambda e: e.tensor_scalar(
                gcol.v(0, 128, 0, 1), gcol.v(0, 128, 1, 2), -1.0, None, ALU.mult),
                reads=(gcol,), writes=(gcol,), tag="nm1"))
            th.append(lambda: P.emit("V", lambda e: e.tensor_scalar(
                g8c.ap, g8a.ap, gcol.v(0, 128, 1, 2), None, ALU.is_equal),
                reads=(g8a, gcol), writes=(g8c,), tag="eqm"))
            th.append(lambda: P.emit("V", lambda e: e.scalar_tensor_tensor(
                g8b.ap, g8c.ap, -1e30, g8a.ap, ALU.mult, ALU.add),
                reads=(g8a, g8c), writes=(g8b,), tag="umask"))
            th.append(lambda: P.emit("V", lambda e: e.tensor_reduce(
                gcol.v(0, 128, 2, 3), g8b.ap, axis=AX.X, op=ALU.max),
                reads=(g8b,), writes=(gcol,), tag="m2p"))
            th.append(lambda: P.emit("V", lambda e: e.tensor_reduce(
                gcol.v(0, 128, 3, 4), g8c.ap, axis=AX.X, op=ALU.add),
                reads=(g8c,), writes=(gcol,), tag="cnt"))
            th.append(lambda: P.emit("V", lambda e: e.tensor_scalar(
                gcol.v(0, 128, 4, 5), gcol.v(0, 128, 3, 4), 1.5, None, ALU.is_ge),
                reads=(gcol,), writes=(gcol,), tag="is2"))
            th.append(lambda: P.emit("V", lambda e: e.tensor_tensor(
                gcol.v(0, 128, 3, 4), gcol.v(0, 128, 1, 2), gcol.v(0, 128, 2, 3),
                ALU.subtract), reads=(gcol,), writes=(gcol,), tag="m2d"))
            th.append(lambda: P.emit("V", lambda e: e.scalar_tensor_tensor(
                gcol.v(0, 128, 5, 6), gcol.v(0, 128, 3, 4), gcol.v(0, 128, 4, 5),
                gcol.v(0, 128, 2, 3), ALU.mult, ALU.add),
                reads=(gcol,), writes=(gcol,), tag="m2"))
            th.append(lambda: P.emit("A", lambda e: e.activation(
                g8b.ap, g8a.ap, AF.Exp, bias=gcol.v(0, 128, 0, 1)),
                reads=(g8a, gcol), writes=(g8b,), tag="eexp"))
            th.append(lambda: P.emit("V", lambda e: e.tensor_scalar(
                g8c.ap, g8a.ap, gcol.v(0, 128, 5, 6), None, ALU.is_ge),
                reads=(g8a, gcol), writes=(g8c,), tag="gemask"))
            th.append(lambda: P.emit("V", lambda e: e.tensor_tensor(
                g8b.ap, g8b.ap, g8c.ap, ALU.mult),
                reads=(g8b, g8c), writes=(g8b,), tag="ew"))
            th.append(lambda: P.emit("V", lambda e: e.tensor_reduce(
                gcol.v(0, 128, 6, 7), g8b.ap, axis=AX.X, op=ALU.add),
                reads=(g8b,), writes=(gcol,), tag="z2"))
            th.append(lambda: P.emit("V", lambda e: e.reciprocal(
                gcol.v(0, 128, 7, 8), gcol.v(0, 128, 6, 7)),
                reads=(gcol,), writes=(gcol,), tag="rz2"))
            th.append(lambda: P.emit("V", lambda e: e.tensor_scalar(
                g8b.ap, g8b.ap, gcol.v(0, 128, 7, 8), None, ALU.mult),
                reads=(g8b, gcol), writes=(g8b,), tag="wef"))
            def t_weT(i=i):
                b2_ = next_bank()
                P.emit("T", lambda e, b2_=b2_: e.transpose(
                    b2_.v(0, 8, 0, 128), g8b.ap, identf.ap),
                    reads=[g8b, identf, b2_], writes=(b2_,), tag="weT")
                P.emit("V", lambda e, b2_=b2_, i=i: e.tensor_copy(
                    weT.v(0, 8, 128 * i, 128 * i + 128), b2_.v(0, 8, 0, 128)),
                    reads=(b2_,), writes=(weTc[i // 4],), tag="weTe")
            th.append(t_weT)
            return th

        hmat = [pb.alloc(128, T, f"hm{m}") for m in range(2)]
        hmc = [[Slot(AR_B, hmat[m].off + 512 * n, 128, 512, f"hmc{m}{n}")
                for n in range(4)] for m in range(2)]

        def h_thunk(m, n):
            def t():
                b_ = next_bank()
                for k in range(NDC):
                    mm(b_, lambda b_=b_: b_.v(0, 128, 0, 512),
                       lambda k=k, m=m: w1catT[k].v(0, 128, 128 * m, 128 * m + 128),
                       lambda k=k, n=n: vd[k].v(0, 128, 512 * n, 512 * n + 512),
                       start=(k == 0), stop=False, reads=(w1catT[k], vd[k]), tag="hMM")
                mm(b_, lambda b_=b_: b_.v(0, 128, 0, 512),
                   lambda m=m: b1row.v(0, 1, 128 * m, 128 * m + 128),
                   lambda: ones512.ap,
                   start=False, stop=True, reads=(b1row, ones512), tag="hAug")
                P.emit("A", lambda e, b_=b_, m=m, n=n: e.activation(
                    hmat[m].v(0, 128, 512 * n, 512 * n + 512), b_.v(0, 128, 0, 512),
                    AF.Silu), reads=(b_,), writes=(hmc[m][n],), tag="silu")
            return t
        h_thunks = [h_thunk(m, n) for n in range(4) for m in range(2)]

        web_t = pb.alloc(128, T, "webcast0")
        web_t2 = pb.alloc(128, T, "webcast1")
        wtc = [[Slot(AR_B, wt.off + 512 * n, 128, 512, f"wtc{m}{n}")
                for n in range(4)] for m, wt in enumerate((web_t, web_t2))]

        R4b = pb.alloc(128, T, "R4b")
        M4b = pb.alloc(128, T, "M4b")
        R4c = [Slot(AR_B, R4b.off + 512 * n, 128, 512, f"R4c{n}") for n in range(4)]
        M4c = [Slot(AR_B, M4b.off + 512 * n, 128, 512, f"M4c{n}") for n in range(4)]
        x3s = [pf.alloc(128, 512, f"x3s{i}") for i in range(2)]
        tsc = [pb.alloc(128, 512, f"tsc{i}") for i in range(2)]
        out_dmas = []

        def ln4_group(n):
            f0, f1 = 512 * n, 512 * n + 512
            mb_ = next_bank()
            qb_ = next_bank()
            for c in range(NDC):
                sq = sqstat[c % 2]
                P.emit("V", lambda e, c=c, sq=sq, f0=f0, f1=f1: e.tensor_tensor(
                    sq.v(0, 128, f0, f1), vd[c].v(0, 128, f0, f1),
                    vd[c].v(0, 128, f0, f1), ALU.mult),
                    reads=(vd[c],), writes=(sq,), tag="l4sq")
                mm(mb_, lambda mb_=mb_: mb_.v(0, 1, 0, 512),
                   lambda: oc1024.v(0, 128),
                   lambda c=c, f0=f0, f1=f1: vd[c].v(0, 128, f0, f1),
                   start=(c == 0), stop=(c == NDC - 1),
                   reads=(oc1024, vd[c]), tag="l4mean")
                mm(qb_, lambda qb_=qb_: qb_.v(0, 1, 0, 512),
                   lambda: oc1024.v(0, 128),
                   lambda sq=sq, f0=f0, f1=f1: sq.v(0, 128, f0, f1),
                   start=(c == 0), stop=(c == NDC - 1),
                   reads=(oc1024, sq), tag="l4msq")
            ev(lambda f0=f0, f1=f1: rowA.v(0, 1, f0, f1), mb_, 0, 1, 512,
               (rowA,), "l4mev")
            ev(lambda f0=f0, f1=f1: rowB.v(0, 1, f0, f1), qb_, 0, 1, 512,
               (rowB,), "l4sev")
            P.emit("V", lambda e, f0=f0, f1=f1: e.tensor_tensor(
                rowC.v(0, 1, f0, f1), rowA.v(0, 1, f0, f1), rowA.v(0, 1, f0, f1),
                ALU.mult), reads=(rowA,), writes=(rowC,), tag="l4m2", special=True)
            P.emit("V", lambda e, f0=f0, f1=f1: e.tensor_tensor(
                rowB.v(0, 1, f0, f1), rowB.v(0, 1, f0, f1), rowC.v(0, 1, f0, f1),
                ALU.subtract), reads=(rowB, rowC), writes=(rowB,), tag="l4var",
                special=True)
            P.emit("A", lambda e, f0=f0, f1=f1: e.activation(
                rowB.v(0, 1, f0, f1), rowB.v(0, 1, f0, f1), AF.Ln,
                bias=epsc.v(0, 1, 0, 1)),
                reads=(rowB, epsc), writes=(rowB,), tag="l4ln", special=True)
            P.emit("A", lambda e, f0=f0, f1=f1: e.activation(
                rowB.v(0, 1, f0, f1), rowB.v(0, 1, f0, f1), AF.Exp, scale=-0.5),
                reads=(rowB,), writes=(rowB,), tag="l4exp", special=True)
            P.emit("V", lambda e, f0=f0, f1=f1: e.tensor_tensor(
                rowC.v(0, 1, f0, f1), rowA.v(0, 1, f0, f1), rowB.v(0, 1, f0, f1),
                ALU.mult), reads=(rowA, rowB), writes=(rowC,), tag="l4mr",
                special=True)
            bb1 = next_bank()
            mm(bb1, lambda bb1=bb1: bb1.v(0, 128, 0, 512),
               lambda: onesf.v(0, 1, 0, 128),
               lambda f0=f0, f1=f1: rowB.v(0, 1, f0, f1),
               start=True, stop=True, reads=(onesf, rowB), tag="l4rbc")
            ev(lambda n=n: R4c[n].ap, bb1, 0, 128, 512, (R4c[n],), "l4rbe")
            bb2 = next_bank()
            mm(bb2, lambda bb2=bb2: bb2.v(0, 128, 0, 512),
               lambda: onesf.v(0, 1, 0, 128),
               lambda f0=f0, f1=f1: rowC.v(0, 1, f0, f1),
               start=True, stop=True, reads=(onesf, rowC), tag="l4mbc")
            ev(lambda n=n: M4c[n].ap, bb2, 0, 128, 512, (M4c[n],), "l4mbe")
            for m in range(NDC):
                ts = tsc[m % 2]
                xs = x3s[m % 2]
                P.emit("V", lambda e, m=m, ts=ts, n=n, f0=f0, f1=f1:
                       e.tensor_tensor(ts.ap, vd[m].v(0, 128, f0, f1),
                                       R4c[n].ap, ALU.mult),
                       reads=(vd[m], R4c[n]), writes=(ts,), tag="x3a")
                P.emit("V", lambda e, ts=ts, n=n: e.tensor_tensor(
                    ts.ap, ts.ap, M4c[n].ap, ALU.subtract),
                    reads=(ts, M4c[n]), writes=(ts,), tag="x3b")
                P.emit("V", lambda e, m=m, xs=xs, ts=ts, f0=f0, f1=f1:
                       e.tensor_tensor(xs.ap, ts.ap, x2[m].v(0, 128, f0, f1),
                                       ALU.add),
                       reads=(ts, x2[m]), writes=(xs,), tag="x3c")
                idx = P.emit("Y" if m % 2 else "G",
                             lambda e, m=m, xs=xs, f0=f0, f1=f1: e.dma_start(
                                 out=d_out[128 * m:128 * m + 128, f0:f1],
                                 in_=xs.ap),
                             reads=(xs,), writes=(), dma=True, tag="ost",
                             tail=True)
                out_dmas.append(idx)

        def tail_group(n):
            for m, wt in enumerate((web_t, web_t2)):
                b_ = next_bank()
                mm(b_, lambda b_=b_: b_.v(0, 128, 0, 512),
                   lambda m=m: selb.v(0, E_, 128 * m, 128 * m + 128),
                   lambda n=n: weT.v(0, E_, 512 * n, 512 * n + 512),
                   start=True, stop=True, reads=(selb, weTc[n]), tag="webc")
                ev(lambda wt=wt, n=n: wt.v(0, 128, 512 * n, 512 * n + 512), b_,
                   0, 128, 512, (wtc[m][n],), "webce")
            for m in range(2):
                P.emit("V", lambda e, m=m, n=n: e.tensor_tensor(
                    hmat[m].v(0, 128, 512 * n, 512 * n + 512),
                    hmat[m].v(0, 128, 512 * n, 512 * n + 512),
                    wtc[m][n].ap, ALU.mult),
                    reads=(hmc[m][n], wtc[m][n]), writes=(hmc[m][n],), tag="hw")
            for m in range(NDC):
                b_ = next_bank()
                for k in range(2):
                    mm(b_, lambda b_=b_: b_.v(0, 128, 0, 512),
                       lambda k=k, m=m: w2cat[k].v(0, 128, 128 * m, 128 * m + 128),
                       lambda k=k, n=n: hmat[k].v(0, 128, 512 * n, 512 * n + 512),
                       start=(k == 0), stop=False, reads=(w2cat[k], hmc[k][n]),
                       tag="dMM")
                mm(b_, lambda b_=b_: b_.v(0, 128, 0, 512),
                   lambda m=m: b2b.v(0, E_, 128 * m, 128 * m + 128),
                   lambda n=n: weT.v(0, E_, 512 * n, 512 * n + 512),
                   start=False, stop=True, reads=(b2b, weTc[n]), tag="dB2")
                P.emit("V", lambda e, b_=b_, m=m, n=n: e.tensor_tensor(
                    vd[m].v(0, 128, 512 * n, 512 * n + 512),
                    b_.v(0, 128, 0, 512),
                    vd[m].v(0, 128, 512 * n, 512 * n + 512), ALU.add),
                    reads=(b_, vd[m]), writes=(vd[m],), tag="ffnadd")
            ln4_group(n)

        for i0_ in range(0, NT, 4):
            ts_ = [gating_thunks(i0_ + kk, gbuf[kk]) for kk in range(4)]
            for x in range(max(len(t) for t in ts_)):
                for t in ts_:
                    if x < len(t):
                        t[x]()
                if x % 4 == 1 and h_thunks:
                    h_thunks.pop(0)()
        while h_thunks:
            h_thunks.pop(0)()
        P.hard_all = bool(int(os.environ.get("KHARD", "0")))
        for n in range(4):
            tail_group(n)
        tap("hm0", hmat[0], 128, T)
        tap("ffnc0", vd[0], 128, T)

        P.emit("Y", lambda e: e.nop(),
               reads=(), writes=(), tag="final")
        P.items[-1]["deps"].update(out_dmas)

        P.finalize(block, sems)

    if _DEBUG:
        print(f"[build] items={len(P.items)} pb_peak={pb.peak} pf_peak={pf.peak}")
    return nc


def _host_prep(inputs):
    C = _host_consts()
    wvdec = np.asarray(inputs["L0_attn_W_vdec"], np.float32)
    wv2 = np.zeros((KVR, NH * 65), np.float32)
    wv2.reshape(KVR, NH, 65)[:, :, :64] = wvdec.reshape(KVR, NH, HD)
    Wg = np.asarray(inputs["L0_ffn_Wg"], np.float32)
    Wp = np.asarray(inputs["L0_ffn_Wp"], np.float32)
    gaterhs = np.concatenate([np.ascontiguousarray(Wg.T),
                              Wp.transpose(2, 0, 1).reshape(D, 64)], axis=1)
    bg72 = np.zeros((1, 72), np.float32)
    bg72[0, :8] = np.asarray(inputs["L0_ffn_bg"], np.float32)
    grcb = np.tile(np.concatenate([
        np.asarray(inputs["L0_ffn_g_r"], np.float32),
        np.asarray(inputs["L0_ffn_b_r"], np.float32),
        np.asarray(inputs["L0_ffn_g_c"], np.float32),
        np.asarray(inputs["L0_ffn_b_c"], np.float32)])[None, :], (128, 1))
    shared = {
        "wkv": np.ascontiguousarray(inputs["L0_attn_W_kv"], np.float32),
        "wq": np.ascontiguousarray(inputs["L0_attn_W_q"], np.float32),
        "wkrope": np.ascontiguousarray(inputs["L0_attn_W_krope"], np.float32),
        "wqrope": np.ascontiguousarray(inputs["L0_attn_W_qrope"], np.float32),
        "wkdec": np.ascontiguousarray(inputs["L0_attn_W_kdec"], np.float32),
        "wv2": wv2,
        "wqdec": np.ascontiguousarray(inputs["L0_attn_W_qdec"], np.float32),
        "wo": np.ascontiguousarray(inputs["L0_attn_W_o"], np.float32),
        "gkv": np.ascontiguousarray(
            np.asarray(inputs["L0_attn_g_kv"], np.float32).reshape(KVR, 1)),
        "wvT": np.ascontiguousarray(np.asarray(inputs["L0_ffn_Wv"], np.float32).T),
        "bv": np.ascontiguousarray(
            np.asarray(inputs["L0_ffn_bv"], np.float32).reshape(1, D)),
        "gaterhs": np.ascontiguousarray(gaterhs),
        "bg72": bg72,
        "w1catT": np.ascontiguousarray(
            np.asarray(inputs["L0_ffn_W1"], np.float32).transpose(2, 0, 1).reshape(D, 256)),
        "b1row": np.ascontiguousarray(
            np.asarray(inputs["L0_ffn_b1"], np.float32).reshape(1, 256)),
        "w2cat": np.ascontiguousarray(
            np.asarray(inputs["L0_ffn_W2"], np.float32).transpose(0, 2, 1).reshape(256, D)),
        "b2": np.ascontiguousarray(inputs["L0_ffn_b2"], np.float32),
        "grcb": np.ascontiguousarray(grcb),
        "alphacol": np.tile(
            np.asarray(inputs["L0_ffn_alpha"], np.float32).reshape(1, 1), (128, 1)),
        "cosd": C["cosd"], "sind": C["sind"], "ident": C["ident"],
        "dmask": C["dmask"], "sel": C["sel"],
    }
    x = np.asarray(inputs["L0_attn_x"], np.float32)
    in_maps = []
    for i in range(8):
        m = dict(shared)
        m["xT"] = np.ascontiguousarray(x[i].T)
        in_maps.append(m)
    return in_maps


def kernel(**inputs):
    in_maps = _host_prep(inputs)
    nc = build_program(debug=_DEBUG)
    res = run_bass_kernel_spmd(nc, in_maps, core_ids=list(range(8)))
    out = np.stack([np.asarray(res.results[i]["out"]).T for i in range(8)], axis=0)
    if _DEBUG:
        kernel._last = res
    return out.astype(np.float32)


# version 25 (final)
# speedup vs baseline: 1.0511x; 1.0511x over previous
import math
import os
import numpy as np
import concourse.bass as bass
import concourse.mybir as mybir
from concourse.bass_utils import run_bass_kernel_spmd

F32 = mybir.dt.float32
BF16 = mybir.dt.bfloat16
AF = mybir.ActivationFunctionType
ALU = mybir.AluOpType
AX = mybir.AxisListType

T, D = 2048, 1024
NH, HD, RD, KVR, QR = 16, 64, 32, 256, 512
NT, NDC = T // 128, D // 128
SCALE = float(np.float32(1.0) / np.sqrt(np.float32(96.0)))
LN8H = float(-0.5 * math.log(8.0))
E_, INNER = 8, 32

_DEBUG = False

ENGS = ("T", "V", "A", "G", "Y")
ENG_ATTR = {"T": "tensor", "V": "vector", "A": "scalar", "G": "gpsimd", "Y": "sync"}


def _host_consts():
    inv_freq = 1.0 / 10000.0 ** (np.arange(0, RD, 2, dtype=np.float32) / RD)
    t = np.arange(T, dtype=np.float32)
    freqs = np.outer(t, inv_freq)
    emb = np.concatenate([freqs, freqs], -1)
    cos = np.cos(emb).astype(np.float32)
    sin = np.sin(emb).astype(np.float32)
    cos_d = np.tile(np.ascontiguousarray(cos.T), (4, 1))
    sin_s = np.ascontiguousarray(sin.T).copy()
    sin_s[:RD // 2, :] *= -1.0
    sin_d = np.tile(sin_s, (4, 1))
    ident = np.eye(128, dtype=np.float32)
    dmask = np.triu(np.ones((128, 128), dtype=np.float32))
    sel = np.zeros((8, 256), dtype=np.float32)
    for e in range(8):
        sel[e, 32 * e:32 * e + 32] = 1.0
    return dict(cosd=np.ascontiguousarray(cos_d), sind=np.ascontiguousarray(sin_d),
                ident=ident, dmask=np.ascontiguousarray(dmask), sel=sel)


class Slot:
    __slots__ = ("arena", "off", "p", "f", "name")

    def __init__(self, arena, off, p, f, name):
        self.arena, self.off, self.p, self.f, self.name = arena, off, p, f, name

    def v(self, p0=0, p1=None, f0=0, f1=None):
        p1 = self.p if p1 is None else p1
        f1 = self.f if f1 is None else f1
        return self.arena[p0:p1, self.off + f0:self.off + f1]

    @property
    def ap(self):
        return self.v()


class Pool:
    def __init__(self, arena, total, name):
        self.arena, self.name, self.total = arena, name, total
        self.free_list = [(0, total)]
        self.pending = []
        self.peak = 0

    def alloc(self, p, f, name=""):
        for i, (off, sz) in enumerate(self.free_list):
            if sz >= f:
                if sz == f:
                    self.free_list.pop(i)
                else:
                    self.free_list[i] = (off + f, sz - f)
                self.peak = max(self.peak, self.total - sum(s for _, s in self.free_list))
                return Slot(self.arena, off, p, f, name)
        used = self.total - sum(sz for _, sz in self.free_list)
        raise MemoryError(f"pool {self.name} OOM for {name} [{p},{f}] "
                          f"used={used}/{self.total} free={self.free_list}")

    def free(self, *slots):
        self.pending.extend(slots)

    def reclaim(self):
        for s in self.pending:
            self.free_list.append((s.off, s.f))
        self.pending = []
        self.free_list.sort()
        merged = []
        for off, sz in self.free_list:
            if merged and merged[-1][0] + merged[-1][1] == off:
                merged[-1] = (merged[-1][0], merged[-1][1] + sz)
            else:
                merged.append([off, sz])
        self.free_list = [tuple(x) for x in merged]


class Bank:
    __slots__ = ("tensor", "foff", "name")

    def __init__(self, tensor, foff, name):
        self.tensor, self.foff, self.name = tensor, foff, name

    def v(self, p0=0, p1=128, f0=0, f1=512):
        return self.tensor[p0:p1, self.foff + f0:self.foff + f1]


class Prog:
    def __init__(self):
        self.items = []
        self.res_w = {}
        self.res_r = {}
        self.bar = None
        self.bar_seen = set()
        self.hard_all = False

    def emit(self, eng, fn, reads=(), writes=(), dma=False, tag="",
             special=False, wwidth=512, tail=False):
        hardp = special or wwidth < 512 or self.hard_all
        deps = set()
        hard = set()
        for r in reads:
            w = self.res_w.get(id(r))
            if w is not None:
                deps.add(w)
                if self.items[w]["hardp"]:
                    hard.add(w)
        for w_ in writes:
            w = self.res_w.get(id(w_))
            if w is not None:
                deps.add(w)
                if self.items[w]["hardp"]:
                    hard.add(w)
            deps.update(self.res_r.get(id(w_), ()))
        key = (eng, dma)
        if self.bar is not None and key not in self.bar_seen:
            deps.update(self.bar)
            self.bar_seen.add(key)
        idx = len(self.items)
        self.items.append(dict(eng=eng, fn=fn, deps=deps, hard=hard, dma=dma,
                               tag=tag, inc=False, val=None, hardp=hardp,
                               tail=tail))
        for r in reads:
            self.res_r.setdefault(id(r), []).append(idx)
        for w_ in writes:
            self.res_w[id(w_)] = idx
            self.res_r[id(w_)] = []
        return idx

    def barrier(self):
        last = {}
        for i, it in enumerate(self.items):
            last[(it["eng"], it["dma"])] = i
        self.bar = sorted(last.values())
        self.bar_seen = set()

    def finalize(self, block, sems):
        items = self.items
        for it in items:
            for d in it["deps"]:
                dit = items[d]
                if dit["dma"] or it["dma"] or dit["eng"] != it["eng"] \
                        or d in it["hard"]:
                    dit["inc"] = True
        for it in items:
            if it["dma"]:
                it["inc"] = True
        counters = {}
        dma_idx = {}
        for it in items:
            if it["dma"]:
                i = dma_idx.get(it["eng"], 0)
                dma_idx[it["eng"]] = i + 1
                it["lane"] = i % 4
                it["lidx"] = i // 4
                it["val"] = 16 * (i // 4 + 1)
            elif it["inc"]:
                key = (it["eng"], False)
                counters[key] = counters.get(key, 0) + 1
                it["val"] = counters[key]
        for eng in ENGS:
            stream = [it for it in items if it["eng"] == eng]
            if not stream:
                continue

            def body(e, stream=stream, eng=eng):
                water = {}
                for it in stream:
                    waits = {}
                    for d in it["deps"]:
                        dit = items[d]
                        same = (dit["eng"] == eng and not dit["dma"] and not it["dma"]
                                and d not in it["hard"])
                        if same or dit["val"] is None:
                            continue
                        if dit["dma"]:
                            skey = (dit["eng"], True, dit["lane"])
                        else:
                            skey = (dit["eng"], False)
                        waits[skey] = max(waits.get(skey, 0), dit["val"])
                    for skey in sorted(waits, key=str):
                        val = waits[skey]
                        if water.get(skey, 0) < val:
                            e.wait_ge(sems[skey], val)
                            water[skey] = val
                    if it["dma"] and not it["tail"] and it["lidx"] > 0:
                        skey = (eng, True, it["lane"])
                        prev = 16 * it["lidx"]
                        if water.get(skey, 0) < prev:
                            e.wait_ge(sems[skey], prev)
                            water[skey] = prev
                    inst = it["fn"](e)
                    if it["dma"]:
                        inst.then_inc(sems[(eng, True, it["lane"])], 16)
                    elif it["inc"]:
                        assert inst is not None, f"{it['tag']} returned None"
                        inst.then_inc(sems[(eng, False)], 1)
            getattr(block, ENG_ATTR[eng])(body)


def build_program(debug=False):
    nc = bass.Bass("TRN2", target_bir_lowering=False, debug=False,
                   detect_race_conditions=False)

    di = {}

    def dram_in(name, shape):
        di[name] = nc.dram_tensor(name, list(shape), F32, kind="ExternalInput").ap()
        return di[name]

    d_xT = dram_in("xT", (D, T))
    d_wkv = dram_in("wkv", (D, KVR))
    d_wq = dram_in("wq", (D, QR))
    d_wkrope = dram_in("wkrope", (D, QR))
    d_wqrope = dram_in("wqrope", (QR, QR))
    d_wkdec = dram_in("wkdec", (KVR, D))
    d_wv2 = dram_in("wv2", (KVR, NH * 65))
    d_wqdec = dram_in("wqdec", (QR, D))
    d_wo = dram_in("wo", (D, D))
    d_gkv = dram_in("gkv", (KVR, 1))
    d_wvT = dram_in("wvT", (D, D))
    d_bv = dram_in("bv", (1, D))
    d_gaterhs = dram_in("gaterhs", (D, 72))
    d_bg72 = dram_in("bg72", (1, 72))
    d_w1catT = dram_in("w1catT", (D, E_ * INNER))
    d_b1row = dram_in("b1row", (1, E_ * INNER))
    d_w2cat = dram_in("w2cat", (E_ * INNER, D))
    d_b2 = dram_in("b2", (E_, D))
    d_grcb = dram_in("grcb", (128, 32))
    d_alpha = dram_in("alphacol", (128, 1))
    d_cosd = dram_in("cosd", (128, T))
    d_sind = dram_in("sind", (128, T))
    d_ident = dram_in("ident", (128, 128))
    d_dmask = dram_in("dmask", (128, 128))
    d_sel = dram_in("sel", (E_, 256))

    d_out = nc.dram_tensor("out", [D, T], F32, kind="ExternalOutput").ap()
    tap_drams = {}

    P = Prog()
    P.hard_all = bool(int(os.environ.get("KHARD", "0")))
    from contextlib import ExitStack
    with ExitStack() as ctx:
        AR_B = ctx.enter_context(nc.sbuf_tensor("arena_b", [128, 88500], BF16))
        AR_F = ctx.enter_context(nc.sbuf_tensor("arena_f", [128, 8600], F32))
        PSA = ctx.enter_context(nc.psum_tensor("psA", [128, 1024], F32))
        PSB = ctx.enter_context(nc.psum_tensor("psB", [128, 1024], F32))
        PS47 = [ctx.enter_context(nc.psum_tensor(f"ps{i}", [128, 512], F32))
                for i in range(4, 8)]
        sems = {}
        for eng in ENGS:
            sems[(eng, False)] = ctx.enter_context(nc.semaphore(f"c{eng}"))
            for lane in range(4):
                sems[(eng, True, lane)] = ctx.enter_context(
                    nc.semaphore(f"d{eng}{lane}"))
        block = ctx.enter_context(nc.Block())

        pb = Pool(AR_B, 88500, "bf16")
        pf = Pool(AR_F, 8600, "f32")
        banks = [Bank(PSA, 0, "b0"), Bank(PSA, 512, "b1"),
                 Bank(PSB, 0, "b2"), Bank(PSB, 512, "b3"),
                 Bank(PS47[0], 0, "b4"), Bank(PS47[1], 0, "b5"),
                 Bank(PS47[2], 0, "b6"), Bank(PS47[3], 0, "b7")]
        span_t = {0: PSA, 1: PSB}

        def ld(dst, src_ap, tag="ld", p=None):
            pp = dst.p if p is None else p
            P.emit("G", lambda e, d=dst, s=src_ap, pp=pp: e.dma_start(out=d.v(0, pp), in_=s),
                   writes=(dst,), dma=True, tag=tag)

        def ldY(dst, src_ap, tag="ldY", p=None):
            pp = dst.p if p is None else p
            P.emit("Y", lambda e, d=dst, s=src_ap, pp=pp: e.dma_start(out=d.v(0, pp), in_=s),
                   writes=(dst,), dma=True, tag=tag)

        def mm(bank_, out_fn, lhs_fn, rhs_fn, start, stop, reads, tag="mm"):
            P.emit("T", lambda e: e.matmul(out_fn(), lhs_fn(), rhs_fn(),
                                           start=start, stop=stop),
                   reads=list(reads) + [bank_], writes=(bank_,), tag=tag)

        ev_rr = [0]

        def ev(dst_ap_fn, bank_, p0, p1, f1, extra_writes, tag, f0=0):
            small = (p1 - p0) * (f1 - f0) < 65536
            if small:
                eng = "V"
            else:
                ev_rr[0] ^= 1
                eng = "V" if ev_rr[0] else "A"
            if eng == "V":
                P.emit("V", lambda e: e.tensor_copy(dst_ap_fn(), bank_.v(p0, p1, f0, f1)),
                       reads=(bank_,), writes=extra_writes, tag=tag, special=small)
            else:
                P.emit("A", lambda e: e.copy(dst_ap_fn(), bank_.v(p0, p1, f0, f1)),
                       reads=(bank_,), writes=extra_writes, tag=tag, special=small)

        def tap(name, slot, p, f):
            if not debug:
                return
            dr = nc.dram_tensor("tap_" + name, [p, f], F32, kind="ExternalOutput").ap()
            tap_drams[name] = dr
            P.emit("G", lambda e, s=slot, dr=dr: e.dma_start(out=dr, in_=s.v(0, p, 0, f)),
                   reads=(slot,), writes=(), dma=True, tag="tap")

        xb = [pb.alloc(128, T, f"xb{c}") for c in range(NDC)]
        for c in range(NDC):
            ld(xb[c], d_xT[128 * c:128 * c + 128, :], f"ld_xb{c}")

        wkv = [pb.alloc(128, KVR, f"wkv{c}") for c in range(NDC)]
        wq = [pb.alloc(128, QR, f"wq{c}") for c in range(NDC)]
        wkrope = [pb.alloc(128, QR, f"wkr{c}") for c in range(NDC)]
        wqrope = [pb.alloc(128, QR, f"wqr{c}") for c in range(4)]
        wkdec = [pb.alloc(128, D, f"wkdec{c}") for c in range(2)]
        wv2 = [pb.alloc(128, NH * 65, f"wv2{c}") for c in range(2)]
        wqdec = [pb.alloc(128, D, f"wqdec{c}") for c in range(4)]
        cosd = pb.alloc(128, T, "cosd")
        sind = pb.alloc(128, T, "sind")
        dmaskb = pb.alloc(128, 128, "dmaskb")
        for c in range(NDC):
            ld(wkv[c], d_wkv[128 * c:128 * c + 128, :])
        for c in range(NDC):
            ld(wq[c], d_wq[128 * c:128 * c + 128, :])
        for c in range(NDC):
            ld(wkrope[c], d_wkrope[128 * c:128 * c + 128, :])
        for c in range(4):
            ld(wqrope[c], d_wqrope[128 * c:128 * c + 128, :])
        for c in range(2):
            ld(wkdec[c], d_wkdec[128 * c:128 * c + 128, :])
            ld(wv2[c], d_wv2[128 * c:128 * c + 128, :])
        for c in range(4):
            ld(wqdec[c], d_wqdec[128 * c:128 * c + 128, :])
        ld(cosd, d_cosd)
        ld(sind, d_sind)
        ld(dmaskb, d_dmask)

        identf = pf.alloc(128, 128, "identf")
        gkvf = pf.alloc(128, 2, "gkvf")
        ldY(identf, d_ident)
        P.emit("Y", lambda e: e.dma_start(out=gkvf.v(0, 128, 0, 1), in_=d_gkv[0:128, :]),
               writes=(gkvf,), dma=True)
        P.emit("Y", lambda e: e.dma_start(out=gkvf.v(0, 128, 1, 2), in_=d_gkv[128:256, :]),
               writes=(gkvf,), dma=True)

        onesb = pb.alloc(1, 128, "onesb")
        ones512 = pb.alloc(1, 512, "ones512")
        oc1024 = pb.alloc(128, 1, "oc1024")
        oc256 = pb.alloc(128, 1, "oc256")
        oc96 = pb.alloc(128, 1, "oc96")
        P.emit("V", lambda e: e.memset(onesb.ap, 1.0), writes=(onesb,), special=True)
        P.emit("V", lambda e: e.memset(ones512.ap, 1.0), writes=(ones512,), special=True)
        P.emit("V", lambda e: e.memset(oc1024.ap, 1.0 / 1024.0), writes=(oc1024,), special=True)
        P.emit("V", lambda e: e.memset(oc256.ap, 1.0 / 256.0), writes=(oc256,), special=True)
        P.emit("V", lambda e: e.memset(oc96.ap, 1.0), writes=(oc96,), special=True)

        for c in range(2):
            P.emit("V", lambda e, c=c: e.tensor_scalar(
                wkdec[c].ap, wkdec[c].ap, gkvf.v(0, 128, c, c + 1), None, ALU.mult),
                reads=(wkdec[c], gkvf), writes=(wkdec[c],), tag="foldg")
            P.emit("V", lambda e, c=c: e.tensor_scalar(
                wv2[c].ap, wv2[c].ap, gkvf.v(0, 128, c, c + 1), None, ALU.mult),
                reads=(wv2[c], gkvf), writes=(wv2[c],), tag="foldg")
        for c in range(4):
            P.emit("V", lambda e, c=c: e.tensor_scalar(
                wqdec[c].ap, wqdec[c].ap, SCALE, None, ALU.mult),
                reads=(wqdec[c],), writes=(wqdec[c],), tag="folds")

        tap("wkdec0", wkdec[0], 128, D)
        tap("wqdec0", wqdec[0], 128, D)
        tap("wkv0", wkv[0], 128, KVR)
        rowA = pf.alloc(1, T, "rowA")
        rowB = pf.alloc(1, T, "rowB")
        rowC = pf.alloc(1, T, "rowC")
        onesf = pf.alloc(1, 128, "onesf")
        P.emit("V", lambda e: e.memset(onesf.ap, 1.0), writes=(onesf,), special=True)
        epsc = pf.alloc(128, 3, "epsc")
        P.emit("V", lambda e: e.memset(epsc.v(0, 128, 0, 1), 1e-5), writes=(epsc,), special=True)
        P.emit("V", lambda e: e.memset(epsc.v(0, 128, 1, 2), 1e-6), writes=(epsc,), special=True)
        P.emit("V", lambda e: e.memset(epsc.v(0, 128, 2, 3), LN8H), writes=(epsc,), special=True)
        sqstat = [pb.alloc(128, T, f"sqstat{i}") for i in range(2)]

        def ln_stats(chunks, inv_col, eps_col, Rb, MRb, tag):
            nk = len(chunks)
            for c in range(nk):
                sq = sqstat[c % 2]
                P.emit("V", lambda e, c=c, sq=sq: e.tensor_tensor(
                    sq.ap, chunks[c].ap, chunks[c].ap, ALU.mult),
                    reads=(chunks[c],), writes=(sq,), tag=tag + "sq")
                for n in range(4):
                    mm(banks[n], lambda n=n: banks[n].v(0, 1, 0, 512),
                       lambda: inv_col.v(0, 128),
                       lambda c=c, n=n: chunks[c].v(0, 128, 512 * n, 512 * n + 512),
                       start=(c == 0), stop=(c == nk - 1),
                       reads=(inv_col, chunks[c]), tag=tag + "mean")
                    mm(banks[4 + n], lambda n=n: banks[4 + n].v(0, 1, 0, 512),
                       lambda: inv_col.v(0, 128),
                       lambda sq=sq, n=n: sq.v(0, 128, 512 * n, 512 * n + 512),
                       start=(c == 0), stop=(c == nk - 1),
                       reads=(inv_col, sq), tag=tag + "msq")
            for n in range(4):
                ev(lambda n=n: rowA.v(0, 1, 512 * n, 512 * n + 512), banks[n],
                   0, 1, 512, (rowA,), tag + "mev")
                ev(lambda n=n: rowB.v(0, 1, 512 * n, 512 * n + 512), banks[4 + n],
                   0, 1, 512, (rowB,), tag + "sev")
            P.emit("V", lambda e: e.tensor_tensor(rowC.ap, rowA.ap, rowA.ap, ALU.mult),
                   reads=(rowA,), writes=(rowC,), tag=tag + "m2", special=True)
            P.emit("V", lambda e: e.tensor_tensor(rowB.ap, rowB.ap, rowC.ap, ALU.subtract),
                   reads=(rowB, rowC), writes=(rowB,), tag=tag + "var", special=True)
            P.emit("A", lambda e: e.activation(rowB.ap, rowB.ap, AF.Ln,
                                               bias=epsc.v(0, 1, eps_col, eps_col + 1)),
                   reads=(rowB, epsc), writes=(rowB,), tag=tag + "ln", special=True)
            P.emit("A", lambda e: e.activation(rowB.ap, rowB.ap, AF.Exp, scale=-0.5),
                   reads=(rowB,), writes=(rowB,), tag=tag + "exp", special=True)
            P.emit("V", lambda e: e.tensor_tensor(rowC.ap, rowA.ap, rowB.ap, ALU.mult),
                   reads=(rowA, rowB), writes=(rowC,), tag=tag + "mr", special=True)
            for n in range(4):
                mm(banks[n % 2], lambda n=n: banks[n % 2].v(0, 128, 0, 512),
                   lambda: onesf.v(0, 1, 0, 128),
                   lambda n=n: rowB.v(0, 1, 512 * n, 512 * n + 512),
                   start=True, stop=True, reads=(onesf, rowB), tag=tag + "rbc")
                ev(lambda n=n: Rb.v(0, 128, 512 * n, 512 * n + 512), banks[n % 2],
                   0, 128, 512, (Rb,), tag + "rbe")
                mm(banks[2 + n % 2], lambda n=n: banks[2 + n % 2].v(0, 128, 0, 512),
                   lambda: onesf.v(0, 1, 0, 128),
                   lambda n=n: rowC.v(0, 1, 512 * n, 512 * n + 512),
                   start=True, stop=True, reads=(onesf, rowC), tag=tag + "mbc")
                ev(lambda n=n: MRb.v(0, 128, 512 * n, 512 * n + 512), banks[2 + n % 2],
                   0, 128, 512, (MRb,), tag + "mbe")

        R1b = pb.alloc(128, T, "R1b")
        M1b = pb.alloc(128, T, "M1b")
        ln_stats(xb, oc1024, 0, R1b, M1b, "ln1")
        for c in range(NDC):
            P.emit("V", lambda e, c=c: e.tensor_tensor(
                xb[c].ap, xb[c].ap, R1b.ap, ALU.mult),
                reads=(xb[c], R1b), writes=(xb[c],), tag="ln1m")
            P.emit("V", lambda e, c=c: e.tensor_tensor(
                xb[c].ap, xb[c].ap, M1b.ap, ALU.subtract),
                reads=(xb[c], M1b), writes=(xb[c],), tag="ln1s")
        tap("h1c0", xb[0], 128, T)
        pb.free(R1b, M1b)
        P.barrier()
        pb.reclaim()

        kvn = [pb.alloc(128, T, f"kvn{c}") for c in range(2)]
        qc_ = [pb.alloc(128, T, f"qc{c}") for c in range(4)]
        krot = [pb.alloc(128, T, f"krot{c}") for c in range(4)]
        qrot = [pb.alloc(128, T, f"qrot{c}") for c in range(4)]
        bank_rr = [0]

        def next_bank():
            bank_rr[0] = (bank_rr[0] + 1) % 8
            return banks[bank_rr[0]]

        def proj(dst, wch, src, nk, mlo, tag):
            for n in range(4):
                b_ = next_bank()
                for k in range(nk):
                    mm(b_, lambda b_=b_: b_.v(0, 128, 0, 512),
                       lambda k=k: wch[k].v(0, 128, mlo, mlo + 128),
                       lambda k=k, n=n: src[k].v(0, 128, 512 * n, 512 * n + 512),
                       start=(k == 0), stop=(k == nk - 1),
                       reads=(wch[k], src[k]), tag=tag)
                ev(lambda n=n, dst=dst: dst.v(0, 128, 512 * n, 512 * n + 512), b_,
                   0, 128, 512, (dst,), tag + "e")

        for m in range(2):
            proj(kvn[m], wkv, xb, NDC, 128 * m, "kv")
        for m in range(4):
            proj(qc_[m], wq, xb, NDC, 128 * m, "qc")
        for m in range(4):
            proj(krot[m], wkrope, xb, NDC, 128 * m, "kraw")
        for m in range(4):
            proj(qrot[m], wqrope, qc_, 4, 128 * m, "qraw")

        for c in range(2):
            sq = sqstat[c % 2]
            P.emit("V", lambda e, c=c, sq=sq: e.tensor_tensor(
                sq.ap, kvn[c].ap, kvn[c].ap, ALU.mult),
                reads=(kvn[c],), writes=(sq,), tag="rmssq")
            for n in range(4):
                mm(banks[n], lambda n=n: banks[n].v(0, 1, 0, 512),
                   lambda: oc256.v(0, 128),
                   lambda sq=sq, n=n: sq.v(0, 128, 512 * n, 512 * n + 512),
                   start=(c == 0), stop=(c == 1),
                   reads=(oc256, sq), tag="rmsmm")
        for n in range(4):
            ev(lambda n=n: rowA.v(0, 1, 512 * n, 512 * n + 512), banks[n],
               0, 1, 512, (rowA,), "rmsev")
        P.emit("A", lambda e: e.activation(rowA.ap, rowA.ap, AF.Ln,
                                           bias=epsc.v(0, 1, 1, 2)),
               reads=(rowA, epsc), writes=(rowA,), tag="rmsln", special=True)
        P.emit("A", lambda e: e.activation(rowA.ap, rowA.ap, AF.Exp, scale=-0.5),
               reads=(rowA,), writes=(rowA,), tag="rmsexp", special=True)
        RMSb = pb.alloc(128, T, "RMSb")
        for n in range(4):
            mm(banks[n % 2], lambda n=n: banks[n % 2].v(0, 128, 0, 512),
               lambda: onesf.v(0, 1, 0, 128),
               lambda n=n: rowA.v(0, 1, 512 * n, 512 * n + 512),
               start=True, stop=True, reads=(onesf, rowA), tag="rmsbc")
            ev(lambda n=n: RMSb.v(0, 128, 512 * n, 512 * n + 512), banks[n % 2],
               0, 128, 512, (RMSb,), "rmsbe")
        for c in range(2):
            P.emit("V", lambda e, c=c: e.tensor_tensor(
                kvn[c].ap, kvn[c].ap, RMSb.ap, ALU.mult),
                reads=(kvn[c], RMSb), writes=(kvn[c],), tag="rmsap")
        tap("kvnc0", kvn[0], 128, T)

        shmask = list(range(16, 32)) + list(range(0, 16))
        rscr = [(sqstat[0], R1b), (sqstat[1], M1b)]
        rscr = [(sqstat[0], sqstat[1])]
        for gi, group in enumerate((krot, qrot)):
            for c in range(4):
                s = group[c]
                rs1, rs2 = rscr[0]
                if (gi * 4 + c) % 2:
                    rs2, rs1 = rscr[0]
                P.emit("V", lambda e, s=s, rs1=rs1: e.stream_shuffle(rs1.ap, s.ap, shmask),
                       reads=(s,), writes=(rs1,), tag="rpsh")
                P.emit("V", lambda e, s=s, rs2=rs2: e.tensor_tensor(
                    rs2.ap, s.ap, cosd.ap, ALU.mult),
                    reads=(s, cosd), writes=(rs2,), tag="rpc")
                P.emit("V", lambda e, rs1=rs1: e.tensor_tensor(
                    rs1.ap, rs1.ap, sind.ap, ALU.mult),
                    reads=(rs1, sind), writes=(rs1,), tag="rps")
                P.emit("V", lambda e, s=s, rs1=rs1, rs2=rs2: e.tensor_tensor(
                    s.ap, rs2.ap, rs1.ap, ALU.add),
                    reads=(rs1, rs2), writes=(s,), tag="rpa")
        tap("krotc0", krot[0], 128, T)
        tap("qrotc0", qrot[0], 128, T)

        pb.free(*xb, *wkv, *wq, *wkrope, *wqrope, cosd, sind, RMSb, *sqstat)
        P.barrier()
        pb.reclaim()
        pf.reclaim()

        sqs = pb.alloc(128, T, "sqs")
        sqsc = [Slot(AR_B, sqs.off + 512 * n, 128, 512, f"sqc{n}") for n in range(4)]
        vtok_base = pb.alloc(128, NH * 65 * NT, "vtok")
        vtok = [Slot(AR_B, vtok_base.off + NH * 65 * i, 128, NH * 65, f"vt{i}")
                for i in range(NT)]

        def vtok_thunk(i):
            def t():
                for n in range(3):
                    ncols = (512, 512, 16)[n]
                    nlo = 512 * n
                    b_ = asm_bank()
                    for k in range(2):
                        mm(b_, lambda b_=b_, ncols=ncols: b_.v(0, 128, 0, ncols),
                           lambda k=k, i=i: kvn[k].v(0, 128, 128 * i, 128 * i + 128),
                           lambda k=k, nlo=nlo, ncols=ncols: wv2[k].v(0, 128, nlo, nlo + ncols),
                           start=(k == 0), stop=(k == 1),
                           reads=(kvn[k], wv2[k]), tag="vdec")
                    ev(lambda i=i, nlo=nlo, ncols=ncols: vtok[i].v(0, 128, nlo, nlo + ncols),
                       b_, 0, 128, ncols, (vtok[i],), "vdece")
                P.emit("V", lambda e, i=i: e.memset(
                    vtok[i].ap.rearrange("p (h c) -> p h c", h=NH)[:, :, 64:65], 1.0),
                    writes=(vtok[i],), tag="vones", special=True)
            return t


        attnT = [pb.alloc(128, T, f"aT{c}") for c in range(NDC)]
        rK = [pb.alloc(97, T, "rK0"), pb.alloc(97, T, "rK1")]
        rQ = [pb.alloc(97, T, "rQ0"), pb.alloc(97, T, "rQ1")]
        U = [pb.alloc(128, 1024, f"U{i}") for i in range(4)]
        rzf = [pf.alloc(1, 512, "rzf0"), pf.alloc(1, 512, "rzf1")]
        rzb = [pb.alloc(1, 512, "rzb0"), pb.alloc(1, 512, "rzb1")]
        rz_rr = [0]
        RBt = [pb.alloc(64, 512, "RBt0"), pb.alloc(64, 512, "RBt1")]
        nmk2 = pf.alloc(1, 4, "nmk2")
        asm_rr = [0]
        avsets = [(banks[4], banks[5]), (banks[6], banks[7])]
        borrow = [1]
        can_asm = [True]
        g_cur = [0]

        def asm_bank():
            asm_rr[0] ^= 1
            return avsets[borrow[0]][asm_rr[0]]

        def assemble_head(hh):
            Kd, Qd = rK[hh % 2], rQ[hh % 2]
            th = []
            tks, tqs = [], []
            for n in range(4):
                def t_k(n=n, hh=hh, Kd=Kd):
                    b_ = asm_bank()
                    for k in range(2):
                        mm(b_, lambda b_=b_: b_.v(0, 64, 0, 512),
                           lambda k=k, hh=hh: wkdec[k].v(0, 128, 64 * hh, 64 * hh + 64),
                           lambda k=k, n=n: kvn[k].v(0, 128, 512 * n, 512 * n + 512),
                           start=(k == 0), stop=(k == 1),
                           reads=(wkdec[k], kvn[k]), tag="kdec")
                    ev(lambda n=n, Kd=Kd: Kd.v(0, 64, 512 * n, 512 * n + 512), b_,
                       0, 64, 512, (Kd,), "kdeA")
                tks.append(t_k)
                def t_q(n=n, hh=hh, Qd=Qd):
                    b_ = asm_bank()
                    for k in range(4):
                        mm(b_, lambda b_=b_: b_.v(0, 64, 0, 512),
                           lambda k=k, hh=hh: wqdec[k].v(0, 128, 64 * hh, 64 * hh + 64),
                           lambda k=k, n=n: qc_[k].v(0, 128, 512 * n, 512 * n + 512),
                           start=(k == 0), stop=(k == 3),
                           reads=(wqdec[k], qc_[k]), tag="qdec")
                    ev(lambda n=n, Qd=Qd: Qd.v(0, 64, 512 * n, 512 * n + 512), b_,
                       0, 64, 512, (Qd,), "qdeA")
                tqs.append(t_q)
            cidx, rlo = hh // 4, 32 * (hh % 4)
            def t_rope(cidx=cidx, rlo=rlo, Kd=Kd, Qd=Qd):
                P.emit("V", lambda e: e.tensor_copy(
                    Kd.v(64, 96), krot[cidx].v(rlo, rlo + 32)),
                    reads=(krot[cidx],), writes=(Kd,), tag="rcpK")
                P.emit("V", lambda e: e.tensor_scalar(
                    Qd.v(64, 96), qrot[cidx].v(rlo, rlo + 32), SCALE, None,
                    ALU.mult), reads=(qrot[cidx],), writes=(Qd,), tag="rcpQ")
                P.emit("V", lambda e: e.memset(Kd.v(96, 97), 1.0),
                       writes=(Kd,), tag="kones", special=True)
            def t_ksq(Kd=Kd):
                for n in range(4):
                    P.emit("V", lambda e, n=n: e.tensor_tensor(
                        sqsc[n].v(0, 96), Kd.v(0, 96, 512 * n, 512 * n + 512),
                        Kd.v(0, 96, 512 * n, 512 * n + 512), ALU.mult),
                        reads=(Kd,), writes=(sqsc[n],), tag="ksq")
            def sum2(nlo, rcv, tag):
                bb = [asm_bank(), asm_bank()]
                for i in range(2):
                    mm(bb[i], lambda b_=bb[i]: b_.v(0, 1, 0, 512),
                       lambda: oc96.v(0, 96),
                       lambda n=nlo + i: sqsc[n].v(0, 96),
                       start=True, stop=True, reads=(oc96, sqsc[nlo + i]), tag=tag)
                for i in range(2):
                    rcv(nlo + i, bb[i])
            def rcv_k(n, b_):
                ev(lambda n=n: rowA.v(0, 1, 512 * n, 512 * n + 512), b_,
                   0, 1, 512, (rowA,), "kn2e")
            def t_kn2a():
                sum2(0, rcv_k, "kn2")
            def t_kn2b():
                sum2(2, rcv_k, "kn2")
                P.emit("V", lambda e: e.reduce_max(
                    nmk2.v(0, 1, 0, 1), rowA.ap, axis=AX.X),
                    reads=(rowA,), writes=(nmk2,), tag="kn2m", special=True)
                P.emit("A", lambda e: e.activation(
                    nmk2.v(0, 1, 1, 2), nmk2.v(0, 1, 0, 1), AF.Ln),
                    reads=(nmk2,), writes=(nmk2,), tag="mkln", special=True)
                P.emit("A", lambda e: e.activation(
                    nmk2.v(0, 1, 2, 3), nmk2.v(0, 1, 1, 2), AF.Exp, scale=0.5),
                    reads=(nmk2,), writes=(nmk2,), tag="mkexp", special=True)
                P.emit("V", lambda e: e.tensor_scalar(
                    nmk2.v(0, 1, 2, 3), nmk2.v(0, 1, 2, 3), -0.5, None, ALU.mult),
                    reads=(nmk2,), writes=(nmk2,), tag="mkneg", special=True)
            def t_qsq(Qd=Qd):
                for n in range(4):
                    P.emit("V", lambda e, n=n: e.tensor_tensor(
                        sqsc[n].v(0, 96), Qd.v(0, 96, 512 * n, 512 * n + 512),
                        Qd.v(0, 96, 512 * n, 512 * n + 512), ALU.mult),
                        reads=(Qd,), writes=(sqsc[n],), tag="qsq")
            def rcv_q(n, b_, Qd=Qd):
                P.emit("V", lambda e, n=n, b_=b_: e.tensor_scalar(
                    Qd.v(96, 97, 512 * n, 512 * n + 512), b_.v(0, 1, 0, 512),
                    1.0, nmk2.v(0, 1, 2, 3), ALU.add, ALU.mult),
                    reads=(b_, nmk2), writes=(Qd,), tag="bqr", special=True)
            def t_qn2a():
                sum2(0, rcv_q, "qn2")
            def t_qn2b():
                sum2(2, rcv_q, "qn2")
            def mk_kq(a, b):
                def t():
                    a()
                    b()
                return t
            th.extend([mk_kq(a, b) for a, b in zip(tks, tqs)])
            th.extend([t_rope, t_ksq, None, None, t_kn2a, t_kn2b,
                       t_qsq, None, t_qn2a, t_qn2b])
            return th

        deferq = []

        def av_evac(hh, qc, bank_, flip=None):
            r = rz_rr[0]
            rz_rr[0] = (r + 1) % 2
            zf, zb = rzf[r], rzb[r]
            P.emit("A", lambda e, bank_=bank_, zf=zf: e.activation(
                zf.ap, bank_.v(64, 65, 0, 512), AF.Ln),
                reads=(bank_,), writes=(zf,), tag="rz", special=True)
            P.emit("A", lambda e, zf=zf, zb=zb: e.activation(
                zb.ap, zf.ap, AF.Exp, scale=-1.0),
                reads=(zf,), writes=(zb,), tag="rzc", special=True)
            deferq.append((g_cur[0] + 3,
                           lambda: av_evac_now(hh, qc, bank_, zb, r, flip)))

        def av_evac_now(hh, qc, bank_, zb, r, flip=None):
            Rb = RBt[r]
            mm(bank_, lambda bank_=bank_: bank_.v(64, 128, 0, 512),
               lambda: onesb.v(0, 1, 0, 64), lambda zb=zb: zb.ap,
               start=True, stop=True, reads=(onesb, zb), tag="rzbc")
            ev(lambda Rb=Rb: Rb.ap, bank_, 64, 128, 512, (Rb,), "rzbe")
            c, rlo = hh // 2, 64 * (hh % 2)
            P.emit("V", lambda e, bank_=bank_, c=c, rlo=rlo, qc=qc, Rb=Rb:
                   e.tensor_tensor(
                       attnT[c].v(rlo, rlo + 64, 512 * qc, 512 * qc + 512),
                       bank_.v(0, 64, 0, 512), Rb.ap, ALU.mult),
                   reads=(bank_, Rb), writes=(attnT[c],), tag="avn")
            if flip is not None:
                borrow[0] = flip
                can_asm[0] = True

        def pump(k=1):
            for _ in range(k):
                if deferq and g_cur[0] >= deferq[0][0]:
                    deferq.pop(0)[1]()
                elif co and can_asm[0]:
                    t = co.pop(0)
                    if t is not None:
                        t()

        def flush_evacs():
            while deferq:
                deferq.pop(0)[1]()

        def emit_S(hh, qlo, j, gpar, uslot):
            Kh, Qh = rK[hh % 2], rQ[hh % 2]
            qs0 = 128 * j
            glo = 512 * qlo
            spanT = span_t[gpar]
            sbanks = (banks[2 * gpar], banks[2 * gpar + 1])
            sc = max(glo, qs0)
            for qc in (qlo, qlo + 1):
                cl, ch = 512 * qc, 512 * qc + 512
                if ch <= qs0:
                    continue
                scl = max(cl, qs0)
                mm(sbanks[qc - qlo],
                   lambda spanT=spanT, scl=scl, ch=ch, glo=glo:
                   spanT[0:128, scl - glo:ch - glo],
                   lambda Kh=Kh, j=j: Kh.v(0, 97, 128 * j, 128 * j + 128),
                   lambda Qh=Qh, scl=scl, ch=ch: Qh.v(0, 97, scl, ch),
                   start=True, stop=True, reads=(Kh, Qh), tag="S")
            u = U[uslot]
            P.emit("A", lambda e, spanT=spanT, sc=sc, glo=glo, u=u:
                   e.activation(u.v(0, 128, sc - glo, 1024),
                                spanT[0:128, sc - glo:1024], AF.Exp),
                   reads=list(sbanks), writes=(u,), tag="exp")
            if glo <= qs0:
                q0 = qs0 - glo
                P.emit("G", lambda e, u=u, q0=q0: e.tensor_tensor(
                    u.v(0, 128, q0, q0 + 128), u.v(0, 128, q0, q0 + 128),
                    dmaskb.ap, ALU.mult),
                    reads=(u, dmaskb), writes=(u,), tag="dmk")

        def emit_AV(hh, qlo, j, pidx, uslot):
            qs0 = 128 * j
            u = U[uslot]
            for qc in (qlo, qlo + 1):
                cl, ch = 512 * qc, 512 * qc + 512
                if ch <= qs0:
                    continue
                scl = max(cl, qs0)
                b_ = avsets[pidx % 2][qc - qlo]
                glo = 512 * qlo
                if j == 0 and qc == qlo and borrow[0] == pidx % 2:
                    can_asm[0] = False
                mm(b_, lambda b_=b_, scl=scl, cl=cl, ch=ch:
                   b_.v(0, 65, scl - cl, ch - cl),
                   lambda j=j, hh=hh: vtok[j].v(0, 128, 65 * hh, 65 * hh + 65),
                   lambda u=u, scl=scl, ch=ch, glo=glo: u.v(0, 128, scl - glo, ch - glo),
                   start=(j == 0), stop=(j == 4 * qc + 3),
                   reads=(vtok[j], u), tag="AV")
                if j == 4 * qc + 3:
                    av_evac(hh, qc, b_,
                            flip=(pidx % 2) if qc == qlo + 1 else None)

        for i in range(NT):
            vtok_thunk(i)()
        for t in assemble_head(0):
            if t is not None:
                t()
        tap("K0", rK[0], 97, T)
        tap("Q0", rQ[0], 97, T)
        stream = []
        for hh in range(NH):
            for (qlo, jmax) in ((0, 8), (2, 16)):
                pidx = 2 * hh + (1 if qlo else 0)
                for j in range(jmax):
                    stream.append((hh, qlo, j, pidx))
        pend = []
        co = []
        cur_h = -1
        for g, (hh, qlo, j, pidx) in enumerate(stream):
            g_cur[0] = g
            if hh != cur_h:
                while co:
                    t = co.pop(0)
                    if t is not None:
                        t()
                cur_h = hh
                co = list(assemble_head(hh + 1)) if hh < NH - 1 else []
            emit_S(hh, qlo, j, g % 2, g % 4)
            pend.append((hh, qlo, j, pidx, g % 4))
            if len(pend) > 3:
                emit_AV(*pend.pop(0))
            pump(1)
        g_cur[0] += 10
        while pend:
            emit_AV(*pend.pop(0))
        flush_evacs()
        tap("attnT0", attnT[0], 128, T)

        pb.free(*rK, *rQ, *U, vtok_base, *kvn, *qc_, *krot, *qrot, *wkdec, *wqdec,
                *rzb, *RBt, sqs, *wv2)
        pf.free(*rzf, nmk2)
        P.barrier()
        pb.reclaim()
        pf.reclaim()

        wo = [pb.alloc(128, D, f"wo{c}") for c in range(NDC)]
        for c in range(NDC):
            ld(wo[c], d_wo[128 * c:128 * c + 128, :], "ld_wo")
        attnP = [pb.alloc(128, T, f"aP{m}") for m in range(NDC)]
        for m in range(NDC):
            for n in range(4):
                b_ = next_bank()
                for k in range(NDC):
                    mm(b_, lambda b_=b_: b_.v(0, 128, 0, 512),
                       lambda k=k, m=m: wo[k].v(0, 128, 128 * m, 128 * m + 128),
                       lambda k=k, n=n: attnT[k].v(0, 128, 512 * n, 512 * n + 512),
                       start=(k == 0), stop=(k == NDC - 1),
                       reads=(wo[k], attnT[k]), tag="woMM")
                ev(lambda m=m, n=n: attnP[m].v(0, 128, 512 * n, 512 * n + 512), b_,
                   0, 128, 512, (attnP[m],), "woev")
        tap("attnP0", attnP[0], 128, T)

        pb.free(*attnT, *wo)
        P.barrier()
        pb.reclaim()

        wvT = [pb.alloc(128, D, f"wvT{c}") for c in range(NDC)]
        gaterhs = [pb.alloc(128, 72, f"gr{c}") for c in range(NDC)]
        w1catT = [pb.alloc(128, E_ * INNER, f"w1{c}") for c in range(NDC)]
        w2cat = [pb.alloc(128, D, f"w2{c}") for c in range(2)]
        b2b = pb.alloc(E_, D, "b2b")
        bvrow = pb.alloc(1, D, "bvrow")
        b1row = pb.alloc(1, E_ * INNER, "b1row")
        bg72row = pb.alloc(1, 72, "bg72")
        selb = pb.alloc(E_, 256, "selb")
        weT = pb.alloc(E_, T, "weT")
        weTc = [Slot(AR_B, weT.off + 512 * n, E_, 512, f"weTc{n}") for n in range(4)]
        grcbf = pf.alloc(128, 32, "grcbf")
        sigcol = pf.alloc(128, 2, "sigcol")
        xfull = [pb.alloc(128, T, f"xf{c}") for c in range(NDC)]
        for c in range(NDC):
            ld(xfull[c], d_xT[128 * c:128 * c + 128, :], "ld_x2")
        ld(b2b, d_b2, "ld_b2")
        ld(bvrow, d_bv, "ld_bv")
        ld(b1row, d_b1row, "ld_b1")
        ld(bg72row, d_bg72, "ld_bg")
        ld(selb, d_sel, "ld_sel")
        for c in range(NDC):
            ld(wvT[c], d_wvT[128 * c:128 * c + 128, :], "ld_wvT")
        for c in range(NDC):
            ld(gaterhs[c], d_gaterhs[128 * c:128 * c + 128, :], "ld_gr")
        for c in range(NDC):
            ld(w1catT[c], d_w1catT[128 * c:128 * c + 128, :], "ld_w1")
        for c in range(2):
            ld(w2cat[c], d_w2cat[128 * c:128 * c + 128, :], "ld_w2")
        ldY(grcbf, d_grcb, "ld_grcb")
        P.emit("Y", lambda e: e.dma_start(out=sigcol.v(0, 128, 0, 1), in_=d_alpha),
               writes=(sigcol,), dma=True, tag="ld_alpha")
        P.emit("A", lambda e: e.activation(
            sigcol.v(0, 128, 1, 2), sigcol.v(0, 128, 0, 1), AF.Sigmoid),
            reads=(sigcol,), writes=(sigcol,), tag="sig", special=True)

        sqstat[0] = pb.alloc(128, T, "sqstat0b")
        sqstat[1] = pb.alloc(128, T, "sqstat1b")
        R2b = pb.alloc(128, T, "R2b")
        M2b = pb.alloc(128, T, "M2b")
        ln_stats(attnP, oc1024, 0, R2b, M2b, "ln2")
        x2 = [pb.alloc(128, T, f"x2{c}") for c in range(NDC)]
        for c in range(NDC):
            xr = xfull[c]
            tscr = sqstat[c % 2]
            P.emit("V", lambda e, c=c, tscr=tscr: e.tensor_tensor(
                tscr.ap, attnP[c].ap, R2b.ap, ALU.mult),
                reads=(attnP[c], R2b), writes=(tscr,), tag="x2a")
            P.emit("V", lambda e, tscr=tscr: e.tensor_tensor(
                tscr.ap, tscr.ap, M2b.ap, ALU.subtract),
                reads=(tscr, M2b), writes=(tscr,), tag="x2b")
            P.emit("V", lambda e, c=c, xr=xr, tscr=tscr: e.tensor_tensor(
                x2[c].ap, tscr.ap, xr.ap, ALU.add),
                reads=(tscr, xr), writes=(x2[c],), tag="x2c")
        tap("x2c0", x2[0], 128, T)

        pb.free(*attnP, R2b, M2b, *xfull)
        P.barrier()
        pb.reclaim()
        pf.reclaim()

        R3b = pb.alloc(128, T, "R3b")
        M3b = pb.alloc(128, T, "M3b")
        ln_stats(x2, oc1024, 0, R3b, M3b, "ln3")
        h2 = [pb.alloc(128, T, f"h2{c}") for c in range(NDC)]
        for c in range(NDC):
            P.emit("V", lambda e, c=c: e.tensor_tensor(
                h2[c].ap, x2[c].ap, R3b.ap, ALU.mult),
                reads=(x2[c], R3b), writes=(h2[c],), tag="h2a")
            P.emit("V", lambda e, c=c: e.tensor_tensor(
                h2[c].ap, h2[c].ap, M3b.ap, ALU.subtract),
                reads=(h2[c], M3b), writes=(h2[c],), tag="h2b")
        pb.free(R3b, M3b)
        tap("h2c0", h2[0], 128, T)

        vd = [pb.alloc(128, T, f"vd{m}") for m in range(NDC)]
        for m in range(NDC):
            for n in range(4):
                b_ = next_bank()
                for k in range(NDC):
                    mm(b_, lambda b_=b_: b_.v(0, 128, 0, 512),
                       lambda k=k, m=m: wvT[k].v(0, 128, 128 * m, 128 * m + 128),
                       lambda k=k, n=n: h2[k].v(0, 128, 512 * n, 512 * n + 512),
                       start=(k == 0), stop=False, reads=(wvT[k], h2[k]), tag="vdMM")
                mm(b_, lambda b_=b_: b_.v(0, 128, 0, 512),
                   lambda m=m: bvrow.v(0, 1, 128 * m, 128 * m + 128),
                   lambda: ones512.ap,
                   start=False, stop=True, reads=(bvrow, ones512), tag="vdAug")
                ev(lambda m=m, n=n: vd[m].v(0, 128, 512 * n, 512 * n + 512), b_,
                   0, 128, 512, (vd[m],), "vdev")
        tap("vdc0", vd[0], 128, T)
        pb.free(*h2, *wvT)
        P.barrier()
        pb.reclaim()

        P.hard_all = True
        gbuf = []
        for s_ in range(4):
            gbuf.append(dict(
                gt=pf.alloc(128, 72, f"gt{s_}"),
                g64=pf.alloc(128, 64, f"g64{s_}"),
                g32=pf.alloc(128, 32, f"g32{s_}"),
                g16=pf.alloc(128, 16, f"g16{s_}"),
                g8a=pf.alloc(128, 8, f"g8a{s_}"),
                g8b=pf.alloc(128, 8, f"g8b{s_}"),
                g8c=pf.alloc(128, 8, f"g8c{s_}"),
                gcol=pf.alloc(128, 8, f"gcol{s_}"),
            ))
        web = pb.alloc(128, 8, "web")

        def v3s(slot, e_, q0, q1):
            return slot.ap.rearrange("p (e q) -> p e q", e=e_)[:, :, q0:q1]

        def gating_thunks(i, B):
            gt, g64, g32, g16 = B["gt"], B["g64"], B["g32"], B["g16"]
            g8a, g8b, g8c, gcol = B["g8a"], B["g8b"], B["g8c"], B["gcol"]
            th = []
            b_ = next_bank()
            def t_mm(b_=b_, i=i):
                for k in range(NDC):
                    mm(b_, lambda b_=b_: b_.v(0, 128, 0, 72),
                       lambda k=k, i=i: vd[k].v(0, 128, 128 * i, 128 * i + 128),
                       lambda k=k: gaterhs[k].ap,
                       start=(k == 0), stop=False, reads=(vd[k], gaterhs[k]), tag="gMM")
                mm(b_, lambda b_=b_: b_.v(0, 128, 0, 72),
                   lambda: ones512.v(0, 1, 0, 128),
                   lambda: bg72row.ap,
                   start=False, stop=True, reads=(ones512, bg72row), tag="gAug")
                P.emit("V", lambda e, b_=b_: e.tensor_copy(gt.ap, b_.v(0, 128, 0, 72)),
                       reads=(b_,), writes=(gt,), tag="gev")
            th.append(t_mm)
            th.append(lambda: P.emit("V", lambda e: e.tensor_tensor(
                g64.ap, gt.v(0, 128, 8, 72), gt.v(0, 128, 8, 72), ALU.mult),
                reads=(gt,), writes=(g64,), tag="gsq"))
            th.append(lambda: P.emit("V", lambda e: e.tensor_tensor(
                g32.ap.rearrange("p (e q) -> p e q", e=8),
                v3s(g64, 8, 0, 4), v3s(g64, 8, 4, 8), ALU.add),
                reads=(g64,), writes=(g32,), tag="gt0"))
            th.append(lambda: P.emit("V", lambda e: e.tensor_tensor(
                g16.ap.rearrange("p (e q) -> p e q", e=8),
                v3s(g32, 8, 0, 2), v3s(g32, 8, 2, 4), ALU.add),
                reads=(g32,), writes=(g16,), tag="gt1"))
            th.append(lambda: P.emit("V", lambda e: e.tensor_tensor(
                g8b.ap.rearrange("p (e q) -> p e q", e=8),
                v3s(g16, 8, 0, 1), v3s(g16, 8, 1, 2), ALU.add),
                reads=(g16,), writes=(g8b,), tag="gt2"))
            th.append(lambda: P.emit("A", lambda e: e.activation(g8b.ap, g8b.ap, AF.Ln),
                   reads=(g8b,), writes=(g8b,), tag="gln"))
            th.append(lambda: P.emit("A", lambda e: e.activation(
                g8b.ap, g8b.ap, AF.Exp, scale=0.5, bias=epsc.v(0, 128, 2, 3)),
                reads=(g8b, epsc), writes=(g8b,), tag="gexp"))

            def ln8(src_ap_fn, dst, res_in, tagp):
                th.append(lambda: P.emit("V", lambda e: e.tensor_reduce(
                    gcol.v(0, 128, 0, 1), src_ap_fn(), axis=AX.X, op=ALU.add),
                    reads=(res_in,), writes=(gcol,), tag=tagp + "sum"))
                th.append(lambda: P.emit("V", lambda e: e.tensor_scalar(
                    gcol.v(0, 128, 0, 1), gcol.v(0, 128, 0, 1), 0.125, None, ALU.mult),
                    reads=(gcol,), writes=(gcol,), tag=tagp + "mean"))
                th.append(lambda: P.emit("V", lambda e: e.tensor_tensor(
                    g8c.ap, src_ap_fn(), src_ap_fn(), ALU.mult),
                    reads=(res_in,), writes=(g8c,), tag=tagp + "sq"))
                th.append(lambda: P.emit("V", lambda e: e.tensor_reduce(
                    gcol.v(0, 128, 1, 2), g8c.ap, axis=AX.X, op=ALU.add),
                    reads=(g8c,), writes=(gcol,), tag=tagp + "sumsq"))
                th.append(lambda: P.emit("V", lambda e: e.tensor_scalar(
                    gcol.v(0, 128, 1, 2), gcol.v(0, 128, 1, 2), 0.125, None, ALU.mult),
                    reads=(gcol,), writes=(gcol,), tag=tagp + "msq"))
                th.append(lambda: P.emit("V", lambda e: e.tensor_tensor(
                    gcol.v(0, 128, 2, 3), gcol.v(0, 128, 0, 1), gcol.v(0, 128, 0, 1),
                    ALU.mult), reads=(gcol,), writes=(gcol,), tag=tagp + "m2"))
                th.append(lambda: P.emit("V", lambda e: e.tensor_tensor(
                    gcol.v(0, 128, 1, 2), gcol.v(0, 128, 1, 2), gcol.v(0, 128, 2, 3),
                    ALU.subtract), reads=(gcol,), writes=(gcol,), tag=tagp + "var"))
                th.append(lambda: P.emit("A", lambda e: e.activation(
                    gcol.v(0, 128, 1, 2), gcol.v(0, 128, 1, 2), AF.Ln,
                    bias=epsc.v(0, 128, 0, 1)),
                    reads=(gcol, epsc), writes=(gcol,), tag=tagp + "ln"))
                th.append(lambda: P.emit("A", lambda e: e.activation(
                    gcol.v(0, 128, 1, 2), gcol.v(0, 128, 1, 2), AF.Exp, scale=-0.5),
                    reads=(gcol,), writes=(gcol,), tag=tagp + "rstd"))
                th.append(lambda: P.emit("V", lambda e: e.tensor_scalar(
                    dst.ap, src_ap_fn(), gcol.v(0, 128, 0, 1), gcol.v(0, 128, 1, 2),
                    ALU.subtract, ALU.mult),
                    reads=(res_in, gcol), writes=(dst,), tag=tagp + "app"))
            ln8(lambda: gt.v(0, 128, 0, 8), g8a, gt, "lnr")
            ln8(lambda: g8b.ap, g8c, g8b, "lnc")
            th.append(lambda: P.emit("V", lambda e: e.tensor_tensor(
                g8a.ap, g8a.ap, grcbf.v(0, 128, 0, 8), ALU.mult),
                reads=(g8a, grcbf), writes=(g8a,), tag="sgr"))
            th.append(lambda: P.emit("V", lambda e: e.tensor_tensor(
                g8a.ap, g8a.ap, grcbf.v(0, 128, 8, 16), ALU.add),
                reads=(g8a, grcbf), writes=(g8a,), tag="sbr"))
            th.append(lambda: P.emit("V", lambda e: e.tensor_tensor(
                g8c.ap, g8c.ap, grcbf.v(0, 128, 16, 24), ALU.mult),
                reads=(g8c, grcbf), writes=(g8c,), tag="sgc"))
            th.append(lambda: P.emit("V", lambda e: e.tensor_tensor(
                g8c.ap, g8c.ap, grcbf.v(0, 128, 24, 32), ALU.add),
                reads=(g8c, grcbf), writes=(g8c,), tag="sbc"))
            th.append(lambda: P.emit("V", lambda e: e.tensor_scalar(
                g8c.ap, g8c.ap, sigcol.v(0, 128, 1, 2), None, ALU.mult),
                reads=(g8c, sigcol), writes=(g8c,), tag="ssig"))
            th.append(lambda: P.emit("V", lambda e: e.tensor_tensor(
                g8a.ap, g8a.ap, g8c.ap, ALU.add),
                reads=(g8a, g8c), writes=(g8a,), tag="ssum"))
            th.append(lambda: P.emit("V", lambda e: e.tensor_reduce(
                gcol.v(0, 128, 1, 2), g8a.ap, axis=AX.X, op=ALU.max),
                reads=(g8a,), writes=(gcol,), tag="m1"))
            th.append(lambda: P.emit("V", lambda e: e.tensor_scalar(
                gcol.v(0, 128, 0, 1), gcol.v(0, 128, 1, 2), -1.0, None, ALU.mult),
                reads=(gcol,), writes=(gcol,), tag="nm1"))
            th.append(lambda: P.emit("V", lambda e: e.tensor_scalar(
                g8c.ap, g8a.ap, gcol.v(0, 128, 1, 2), None, ALU.is_equal),
                reads=(g8a, gcol), writes=(g8c,), tag="eqm"))
            th.append(lambda: P.emit("V", lambda e: e.scalar_tensor_tensor(
                g8b.ap, g8c.ap, -1e30, g8a.ap, ALU.mult, ALU.add),
                reads=(g8a, g8c), writes=(g8b,), tag="umask"))
            th.append(lambda: P.emit("V", lambda e: e.tensor_reduce(
                gcol.v(0, 128, 2, 3), g8b.ap, axis=AX.X, op=ALU.max),
                reads=(g8b,), writes=(gcol,), tag="m2p"))
            th.append(lambda: P.emit("V", lambda e: e.tensor_reduce(
                gcol.v(0, 128, 3, 4), g8c.ap, axis=AX.X, op=ALU.add),
                reads=(g8c,), writes=(gcol,), tag="cnt"))
            th.append(lambda: P.emit("V", lambda e: e.tensor_scalar(
                gcol.v(0, 128, 4, 5), gcol.v(0, 128, 3, 4), 1.5, None, ALU.is_ge),
                reads=(gcol,), writes=(gcol,), tag="is2"))
            th.append(lambda: P.emit("V", lambda e: e.tensor_tensor(
                gcol.v(0, 128, 3, 4), gcol.v(0, 128, 1, 2), gcol.v(0, 128, 2, 3),
                ALU.subtract), reads=(gcol,), writes=(gcol,), tag="m2d"))
            th.append(lambda: P.emit("V", lambda e: e.scalar_tensor_tensor(
                gcol.v(0, 128, 5, 6), gcol.v(0, 128, 3, 4), gcol.v(0, 128, 4, 5),
                gcol.v(0, 128, 2, 3), ALU.mult, ALU.add),
                reads=(gcol,), writes=(gcol,), tag="m2"))
            th.append(lambda: P.emit("A", lambda e: e.activation(
                g8b.ap, g8a.ap, AF.Exp, bias=gcol.v(0, 128, 0, 1)),
                reads=(g8a, gcol), writes=(g8b,), tag="eexp"))
            th.append(lambda: P.emit("V", lambda e: e.tensor_scalar(
                g8c.ap, g8a.ap, gcol.v(0, 128, 5, 6), None, ALU.is_ge),
                reads=(g8a, gcol), writes=(g8c,), tag="gemask"))
            th.append(lambda: P.emit("V", lambda e: e.tensor_tensor(
                g8b.ap, g8b.ap, g8c.ap, ALU.mult),
                reads=(g8b, g8c), writes=(g8b,), tag="ew"))
            th.append(lambda: P.emit("V", lambda e: e.tensor_reduce(
                gcol.v(0, 128, 6, 7), g8b.ap, axis=AX.X, op=ALU.add),
                reads=(g8b,), writes=(gcol,), tag="z2"))
            th.append(lambda: P.emit("V", lambda e: e.reciprocal(
                gcol.v(0, 128, 7, 8), gcol.v(0, 128, 6, 7)),
                reads=(gcol,), writes=(gcol,), tag="rz2"))
            th.append(lambda: P.emit("V", lambda e: e.tensor_scalar(
                g8b.ap, g8b.ap, gcol.v(0, 128, 7, 8), None, ALU.mult),
                reads=(g8b, gcol), writes=(g8b,), tag="wef"))
            def t_weT(i=i):
                b2_ = next_bank()
                P.emit("T", lambda e, b2_=b2_: e.transpose(
                    b2_.v(0, 8, 0, 128), g8b.ap, identf.ap),
                    reads=[g8b, identf, b2_], writes=(b2_,), tag="weT")
                P.emit("V", lambda e, b2_=b2_, i=i: e.tensor_copy(
                    weT.v(0, 8, 128 * i, 128 * i + 128), b2_.v(0, 8, 0, 128)),
                    reads=(b2_,), writes=(weTc[i // 4],), tag="weTe")
            th.append(t_weT)
            return th

        hmat = [pb.alloc(128, T, f"hm{m}") for m in range(2)]
        hmc = [[Slot(AR_B, hmat[m].off + 512 * n, 128, 512, f"hmc{m}{n}")
                for n in range(4)] for m in range(2)]

        def h_thunk(m, n):
            def t():
                b_ = next_bank()
                for k in range(NDC):
                    mm(b_, lambda b_=b_: b_.v(0, 128, 0, 512),
                       lambda k=k, m=m: w1catT[k].v(0, 128, 128 * m, 128 * m + 128),
                       lambda k=k, n=n: vd[k].v(0, 128, 512 * n, 512 * n + 512),
                       start=(k == 0), stop=False, reads=(w1catT[k], vd[k]), tag="hMM")
                mm(b_, lambda b_=b_: b_.v(0, 128, 0, 512),
                   lambda m=m: b1row.v(0, 1, 128 * m, 128 * m + 128),
                   lambda: ones512.ap,
                   start=False, stop=True, reads=(b1row, ones512), tag="hAug")
                P.emit("A", lambda e, b_=b_, m=m, n=n: e.activation(
                    hmat[m].v(0, 128, 512 * n, 512 * n + 512), b_.v(0, 128, 0, 512),
                    AF.Silu), reads=(b_,), writes=(hmc[m][n],), tag="silu")
            return t
        h_thunks = [h_thunk(m, n) for n in range(4) for m in range(2)]

        web_t = pb.alloc(128, T, "webcast0")
        web_t2 = pb.alloc(128, T, "webcast1")
        wtc = [[Slot(AR_B, wt.off + 512 * n, 128, 512, f"wtc{m}{n}")
                for n in range(4)] for m, wt in enumerate((web_t, web_t2))]

        def tail_group(n):
            for m, wt in enumerate((web_t, web_t2)):
                b_ = next_bank()
                mm(b_, lambda b_=b_: b_.v(0, 128, 0, 512),
                   lambda m=m: selb.v(0, E_, 128 * m, 128 * m + 128),
                   lambda n=n: weT.v(0, E_, 512 * n, 512 * n + 512),
                   start=True, stop=True, reads=(selb, weTc[n]), tag="webc")
                ev(lambda wt=wt, n=n: wt.v(0, 128, 512 * n, 512 * n + 512), b_,
                   0, 128, 512, (wtc[m][n],), "webce")
            for m in range(2):
                P.emit("V", lambda e, m=m, n=n: e.tensor_tensor(
                    hmat[m].v(0, 128, 512 * n, 512 * n + 512),
                    hmat[m].v(0, 128, 512 * n, 512 * n + 512),
                    wtc[m][n].ap, ALU.mult),
                    reads=(hmc[m][n], wtc[m][n]), writes=(hmc[m][n],), tag="hw")
            for m in range(NDC):
                b_ = next_bank()
                for k in range(2):
                    mm(b_, lambda b_=b_: b_.v(0, 128, 0, 512),
                       lambda k=k, m=m: w2cat[k].v(0, 128, 128 * m, 128 * m + 128),
                       lambda k=k, n=n: hmat[k].v(0, 128, 512 * n, 512 * n + 512),
                       start=(k == 0), stop=False, reads=(w2cat[k], hmc[k][n]),
                       tag="dMM")
                mm(b_, lambda b_=b_: b_.v(0, 128, 0, 512),
                   lambda m=m: b2b.v(0, E_, 128 * m, 128 * m + 128),
                   lambda n=n: weT.v(0, E_, 512 * n, 512 * n + 512),
                   start=False, stop=True, reads=(b2b, weTc[n]), tag="dB2")
                P.emit("V", lambda e, b_=b_, m=m, n=n: e.tensor_tensor(
                    vd[m].v(0, 128, 512 * n, 512 * n + 512),
                    b_.v(0, 128, 0, 512),
                    vd[m].v(0, 128, 512 * n, 512 * n + 512), ALU.add),
                    reads=(b_, vd[m]), writes=(vd[m],), tag="ffnadd")

        for i0_ in range(0, NT, 4):
            ts_ = [gating_thunks(i0_ + kk, gbuf[kk]) for kk in range(4)]
            for x in range(max(len(t) for t in ts_)):
                for t in ts_:
                    if x < len(t):
                        t[x]()
                if x % 4 == 1 and h_thunks:
                    h_thunks.pop(0)()
        while h_thunks:
            h_thunks.pop(0)()
        P.hard_all = bool(int(os.environ.get("KHARD", "0")))
        for n in range(4):
            tail_group(n)
        tap("hm0", hmat[0], 128, T)
        tap("ffnc0", vd[0], 128, T)

        pb.free(*w1catT, *gaterhs, web_t, web_t2)
        P.barrier()
        pb.reclaim()
        pf.reclaim()

        R4b = pb.alloc(128, T, "R4b")
        M4b = pb.alloc(128, T, "M4b")
        ln_stats(vd, oc1024, 0, R4b, M4b, "ln4")
        pf.free(rowA, rowB, rowC)
        P.barrier()
        pf.reclaim()
        x3f2 = [pf.alloc(128, T, f"x3f{i}") for i in range(3)]
        tscr2 = [pb.alloc(128, T, "ts0"), sqstat[1]]
        out_dmas = []
        for m in range(NDC):
            x3f = x3f2[m % 3]
            tscr_ = tscr2[m % 2]
            P.emit("V", lambda e, m=m, tscr_=tscr_: e.tensor_tensor(
                tscr_.ap, vd[m].ap, R4b.ap, ALU.mult),
                reads=(vd[m], R4b), writes=(tscr_,), tag="x3a")
            P.emit("V", lambda e, tscr_=tscr_: e.tensor_tensor(
                tscr_.ap, tscr_.ap, M4b.ap, ALU.subtract),
                reads=(tscr_, M4b), writes=(tscr_,), tag="x3b")
            P.emit("V", lambda e, m=m, x3f=x3f, tscr_=tscr_: e.tensor_tensor(
                x3f.ap, tscr_.ap, x2[m].ap, ALU.add),
                reads=(tscr_, x2[m]), writes=(x3f,), tag="x3c")
            idx = P.emit("Y" if m % 2 else "G",
                         lambda e, m=m, x3f=x3f: e.dma_start(
                             out=d_out[128 * m:128 * m + 128, :], in_=x3f.ap),
                         reads=(x3f,), writes=(), dma=True, tag="ost", tail=True)
            out_dmas.append(idx)

        P.emit("Y", lambda e: e.nop(),
               reads=(), writes=(), tag="final")
        P.items[-1]["deps"].update(out_dmas)

        P.finalize(block, sems)

    if _DEBUG:
        print(f"[build] items={len(P.items)} pb_peak={pb.peak} pf_peak={pf.peak}")
    return nc


def _host_prep(inputs):
    C = _host_consts()
    wvdec = np.asarray(inputs["L0_attn_W_vdec"], np.float32)
    wv2 = np.zeros((KVR, NH * 65), np.float32)
    wv2.reshape(KVR, NH, 65)[:, :, :64] = wvdec.reshape(KVR, NH, HD)
    Wg = np.asarray(inputs["L0_ffn_Wg"], np.float32)
    Wp = np.asarray(inputs["L0_ffn_Wp"], np.float32)
    gaterhs = np.concatenate([np.ascontiguousarray(Wg.T),
                              Wp.transpose(2, 0, 1).reshape(D, 64)], axis=1)
    bg72 = np.zeros((1, 72), np.float32)
    bg72[0, :8] = np.asarray(inputs["L0_ffn_bg"], np.float32)
    grcb = np.tile(np.concatenate([
        np.asarray(inputs["L0_ffn_g_r"], np.float32),
        np.asarray(inputs["L0_ffn_b_r"], np.float32),
        np.asarray(inputs["L0_ffn_g_c"], np.float32),
        np.asarray(inputs["L0_ffn_b_c"], np.float32)])[None, :], (128, 1))
    shared = {
        "wkv": np.ascontiguousarray(inputs["L0_attn_W_kv"], np.float32),
        "wq": np.ascontiguousarray(inputs["L0_attn_W_q"], np.float32),
        "wkrope": np.ascontiguousarray(inputs["L0_attn_W_krope"], np.float32),
        "wqrope": np.ascontiguousarray(inputs["L0_attn_W_qrope"], np.float32),
        "wkdec": np.ascontiguousarray(inputs["L0_attn_W_kdec"], np.float32),
        "wv2": wv2,
        "wqdec": np.ascontiguousarray(inputs["L0_attn_W_qdec"], np.float32),
        "wo": np.ascontiguousarray(inputs["L0_attn_W_o"], np.float32),
        "gkv": np.ascontiguousarray(
            np.asarray(inputs["L0_attn_g_kv"], np.float32).reshape(KVR, 1)),
        "wvT": np.ascontiguousarray(np.asarray(inputs["L0_ffn_Wv"], np.float32).T),
        "bv": np.ascontiguousarray(
            np.asarray(inputs["L0_ffn_bv"], np.float32).reshape(1, D)),
        "gaterhs": np.ascontiguousarray(gaterhs),
        "bg72": bg72,
        "w1catT": np.ascontiguousarray(
            np.asarray(inputs["L0_ffn_W1"], np.float32).transpose(2, 0, 1).reshape(D, 256)),
        "b1row": np.ascontiguousarray(
            np.asarray(inputs["L0_ffn_b1"], np.float32).reshape(1, 256)),
        "w2cat": np.ascontiguousarray(
            np.asarray(inputs["L0_ffn_W2"], np.float32).transpose(0, 2, 1).reshape(256, D)),
        "b2": np.ascontiguousarray(inputs["L0_ffn_b2"], np.float32),
        "grcb": np.ascontiguousarray(grcb),
        "alphacol": np.tile(
            np.asarray(inputs["L0_ffn_alpha"], np.float32).reshape(1, 1), (128, 1)),
        "cosd": C["cosd"], "sind": C["sind"], "ident": C["ident"],
        "dmask": C["dmask"], "sel": C["sel"],
    }
    x = np.asarray(inputs["L0_attn_x"], np.float32)
    in_maps = []
    for i in range(8):
        m = dict(shared)
        m["xT"] = np.ascontiguousarray(x[i].T)
        in_maps.append(m)
    return in_maps


def kernel(**inputs):
    in_maps = _host_prep(inputs)
    nc = build_program(debug=_DEBUG)
    res = run_bass_kernel_spmd(nc, in_maps, core_ids=list(range(8)))
    out = np.stack([np.asarray(res.results[i]["out"]).T for i in range(8)], axis=0)
    if _DEBUG:
        kernel._last = res
    return out.astype(np.float32)
